# version 31
# baseline (speedup 1.0000x reference)
import numpy as np
import ml_dtypes
import concourse.bass as bass
import concourse.mybir as mybir
from concourse.bass_utils import run_bass_kernel_spmd
from concourse.replica_groups import maybe_share_collective_output_space

F32 = mybir.dt.float32
BF16 = mybir.dt.bfloat16
I32 = mybir.dt.int32
I16 = mybir.dt.int16
Alu = mybir.AluOpType
Act = mybir.ActivationFunctionType
AX = mybir.AxisListType.X

NCORES = 8
TC = 2048
D = 1024
DI = 2816
NT = 4
TT = 512
NG = 4
NK = 8
NI = 22
EPS = 1e-5
MASK_HI = -4194304
HALF_DOWN = 0x1FFFFF
ABS_MASK = 0x7FFFFFFF
SIGN_MASK = -2147483648

NSL = [(0, 512), (512, 512), (1024, 512), (1536, 512), (2048, 512), (2560, 256)]
RG = [[0, 1, 2, 3, 4, 5, 6, 7]]

ARN = 34816
O_H = 0
O_XG = 16384
O_XNH = 18432
O_XNL = 20480
O_WZ = 22528
O_ZB = 26624
O_HT = 27648
O_BB = 28672
O_CU = 29184
O_P2 = 29696
P2A = O_P2
P2B = O_P2 + 1024
P2C = O_P2 + 2048
P2M = O_P2 + 3072
P2E = O_P2 + 3584
P2F = O_P2 + 4096
O_D = 16384
O_HTM = 17408
O_F = 19456
O_GAT = 20480
O_SEQ = 21504
O_PRF = 21632
O_XXG = 21760
O_FXG = 0
O_S1 = 2048
O_S2 = 3072
O_S3 = 4096
O_XQH = 5120
O_XQL = 7168
O_WG = 9216
O_HQS = 17408
O_HQTH = 19456
O_HQTL = 25088
O_WD = 30720
O_H2 = 30720
O_GS = 32768

ENGINES = ["sync", "scalar", "vector", "tensor", "gpsimd"]
DEBUG = False


class Plan:
    def __init__(self, nc):
        self.nc = nc
        self.steps = {e: [] for e in ENGINES}
        self.sems = {}
        self.cnt = {}

    def sem(self, name, es):
        h = es.enter_context(self.nc.semaphore(name))
        self.sems[name] = h
        self.cnt[name] = 0

    def step(self, eng, f, inc=None, n=1):
        if inc is not None:
            self.cnt[inc] += n
        self.steps[eng].append(("op", f, inc, n))

    def dma(self, eng, out, in_, inc, **kw):
        self.cnt[inc] += 16
        self.steps[eng].append(
            ("op", (lambda e, o=out, i=in_, k=dict(kw):
                    e.dma_start(out=o, in_=i, **k)), inc, 16))

    def wait(self, eng, sem, val=None):
        v = self.cnt[sem] if val is None else val
        self.steps[eng].append(("wait", sem, v, 0))

    def n(self, sem):
        return self.cnt[sem]

    def emit(self, block):
        for eng in ENGINES:
            steps = self.steps[eng]
            if not steps:
                continue
            deco = getattr(block, eng)

            def make(steps=steps):
                def body(e):
                    for kind, a, b, c in steps:
                        if kind == "wait":
                            e.wait_ge(self.sems[a], b)
                        else:
                            inst = a(e)
                            if b is not None and inst is not None:
                                inst.then_inc(self.sems[b], c)
                return body
            deco(make())


def build_nc():
    nc = bass.Bass("TRN2", target_bir_lowering=False, debug=False,
                   detect_race_conditions=False)

    x_h = nc.dram_tensor("x", [TC, D], F32, kind="ExternalInput").ap()
    wz_h = nc.dram_tensor("wz", [D, D], F32, kind="ExternalInput").ap()
    wh_h = nc.dram_tensor("wh", [D, D], F32, kind="ExternalInput").ap()
    bz_h = nc.dram_tensor("bzp", [128, 8], F32, kind="ExternalInput").ap()
    bh_h = nc.dram_tensor("bhp", [128, 8], F32, kind="ExternalInput").ap()
    ldt_h = nc.dram_tensor("ldt", [128, D], F32, kind="ExternalInput").ap()
    ws1_h = nc.dram_tensor("wsh1", [768, D], F32, kind="ExternalInput").ap()
    ws2_h = nc.dram_tensor("wsh2", [D, 384], F32, kind="ExternalInput").ap()
    idf_h = nc.dram_tensor("idf", [128, 128], F32, kind="ExternalInput").ap()
    idb_h = nc.dram_tensor("idb", [128, 128], BF16, kind="ExternalInput").ap()
    sel_h = nc.dram_tensor("sel", [128, 64], F32, kind="ExternalInput").ap()
    y_h = nc.dram_tensor("y", [TC, D], F32, kind="ExternalOutput").ap()

    wzh_s = nc.dram_tensor("wzh_s", [D, D], BF16, kind="Internal").ap()
    wzl_s = nc.dram_tensor("wzl_s", [D, D], BF16, kind="Internal").ap()
    whh_s = nc.dram_tensor("whh_s", [D, D], BF16, kind="Internal").ap()
    whl_s = nc.dram_tensor("whl_s", [D, D], BF16, kind="Internal").ap()
    sp1 = maybe_share_collective_output_space("AllGather", RG)
    spl = maybe_share_collective_output_space("AllGather", RG,
                                              prefer_local=True)
    ag1a_in = nc.dram_tensor("ag1a_in", [D, 768], BF16, kind="Internal").ap()
    ag1a_out = nc.dram_tensor("ag1a_out", [8 * D, 768], BF16, kind="Internal",
                              addr_space=sp1).ap()
    ag1b_in = nc.dram_tensor("ag1b_in", [D, 768], BF16, kind="Internal").ap()
    ag1b_out = nc.dram_tensor("ag1b_out", [8 * D, 768], BF16, kind="Internal",
                              addr_space=sp1).ap()
    ag2_in = nc.dram_tensor("ag2_in", [384, 2048], BF16, kind="Internal").ap()
    ag2_out = nc.dram_tensor("ag2_out", [3072, 2048], BF16, kind="Internal",
                             addr_space=sp1).ap()
    ccm_in = nc.dram_tensor("ccm_in", [2, D], F32, kind="Internal").ap()
    ccm_out = nc.dram_tensor("ccm_out", [16, D], F32, kind="Internal",
                             addr_space=spl).ap()

    dbg_tensors = {}

    def mkdbg(name, shape, dt=F32):
        dbg_tensors[name] = nc.dram_tensor("dbg_" + name, shape, dt,
                                           kind="ExternalOutput").ap()

    if DEBUG:
        mkdbg("xn0", [128, 1024])
        mkdbg("xnh", [128, 4096], BF16)
        mkdbg("xnl", [128, 4096], BF16)
        mkdbg("z0", [128, 512])
        mkdbg("ht0", [128, 512])
        mkdbg("a_all", [128, 16384])
        mkdbg("h_allM", [128, 16384])
        mkdbg("ast", [128, 8])
        mkdbg("hst", [128, 8])
        mkdbg("fst", [128, 8])
        mkdbg("seq", [128, 128])
        mkdbg("prf", [128, 64])
        mkdbg("h_allX", [128, 16384])
        mkdbg("htm0", [128, 1024])
        mkdbg("fb0", [128, 512])
        mkdbg("yprov", [128, 16384])
        mkdbg("xq0", [128, 1024])
        mkdbg("xqh", [128, 4096], BF16)
        mkdbg("gs0", [128, 2048])
        mkdbg("h20", [128, 2048])
        mkdbg("hqs", [128, 2048], BF16)
        mkdbg("hqth", [128, 11264], BF16)
        mkdbg("agin1", [1024, 1536], BF16)
        mkdbg("agout1", [8192, 1536], BF16)
        mkdbg("agin2", [384, 2048], BF16)
        mkdbg("agout2", [3072, 2048], BF16)
        mkdbg("piece0", [128, 4096], BF16)
        mkdbg("wd0", [128, 2048], BF16)

    from contextlib import ExitStack
    es = ExitStack()
    with es:
        AR_t = es.enter_context(nc.sbuf_tensor("AR", [128, ARN], F32))
        AA_t = es.enter_context(nc.sbuf_tensor("AA", [128, 16384], F32))
        IDF = es.enter_context(nc.sbuf_tensor("IDF", [128, 128], F32))
        IDB = es.enter_context(nc.sbuf_tensor("IDB", [128, 128], BF16))
        BZ = es.enter_context(nc.sbuf_tensor("BZ", [128, 8], F32))
        NBZ = es.enter_context(nc.sbuf_tensor("NBZ", [128, 8], F32))
        BH = es.enter_context(nc.sbuf_tensor("BH", [128, 8], F32))
        SEL = es.enter_context(nc.sbuf_tensor("SEL", [128, 64], F32))
        HST = es.enter_context(nc.sbuf_tensor("HST", [128, 8], F32))
        AST = es.enter_context(nc.sbuf_tensor("AST", [128, 8], F32))
        FST = es.enter_context(nc.sbuf_tensor("FST", [128, 8], F32))
        SC = es.enter_context(nc.sbuf_tensor("SC", [128, 128], F32))
        RC = es.enter_context(nc.sbuf_tensor("RC", [128, 128], F32))
        TN = es.enter_context(nc.sbuf_tensor("TN", [128, 96], F32))
        SMK = es.enter_context(nc.sbuf_tensor("SMK", [128, 1], I32))
        pb = [es.enter_context(nc.psum_tensor(f"pb{i}", [128, 512], F32))
              for i in range(8)]

        AR = AR_t[:]
        AA = AA_t[:]
        AR16 = AR.bitcast(BF16)
        ARI = AR.bitcast(I32)
        ARS = AR.bitcast(I16)

        def arf(o, n):
            return AR[:, o:o + n]

        def arb(o, n):
            return AR16[:, 2 * o:2 * o + n]

        def ari(o, n):
            return ARI[:, o:o + n]

        p = Plan(nc)
        for s in ["qS", "qR", "sP", "sV", "sC", "sG", "fS", "fR", "sY",
                  "qG", "fG"]:
            p.sem(s, es)

        def vsync():
            p.step("vector", lambda e: e.engine_nop(), inc="sY")
            p.wait("vector", "sY")

        QMAP = {"sync": ("qS", "fS"), "scalar": ("qR", "fR"),
                "gpsimd": ("qG", "fG")}

        def qfence(queue):
            qsem, fsem = QMAP[queue]
            p.wait(queue, qsem)
            p.step(queue, lambda e: e.nop(), inc=fsem)
            return p.n(fsem)

        def dbg(name, ap, eng="vector"):
            if not DEBUG or name not in dbg_tensors:
                return
            p.wait("scalar", "sV" if eng == "vector" else
                   ("sC" if eng == "scalar" else "sP"))
            p.dma("scalar", dbg_tensors[name], ap, "qR")

        bank_free = [None] * 8

        def bank_wait(eng, b):
            if bank_free[b] is not None:
                s, v = bank_free[b]
                p.wait(eng, s, v)

        p.dma("sync", IDF[:], idf_h, "qS")
        p.dma("sync", IDB[:], idb_h, "qS")
        p.dma("sync", BZ[:], bz_h, "qS")
        p.dma("sync", BH[:], bh_h, "qS")
        p.dma("sync", SEL[:], sel_h, "qS")
        consts_f = qfence("sync")
        p.wait("tensor", "fS", consts_f)
        p.wait("scalar", "fS", consts_f)
        p.wait("vector", "fS", consts_f)
        p.step("vector", lambda e: e.memset(HST[:], 0.0))
        p.step("vector", lambda e: e.tensor_scalar(
            NBZ[:], BZ[:], -1.0, None, Alu.mult))
        p.step("vector", lambda e: e.memset(AST[:], 1.0))
        p.step("vector", lambda e: e.memset(SMK[:], SIGN_MASK))

        def quant(X, oB, oC, oM, nblk, width, QH=None, QL=None):
            B = arf(oB, width)
            Bi = ari(oB, width)
            C = arf(oC, width)
            Ci = ari(oC, width)
            Mk = ARS[:, 2 * oM:2 * oM + width]
            Xi = X.bitcast(I32)
            B3 = B.rearrange("p (b j) -> p b j", j=16)
            C3 = C.rearrange("p (b j) -> p b j", j=16)
            sc = SC[:, 0:nblk]
            rc = RC[:, 0:nblk]
            sc3 = sc.rearrange("p (b o) -> p b o", o=1)
            rcb = rc.unsqueeze(2).broadcast_to([128, nblk, 16])
            scb = sc.unsqueeze(2).broadcast_to([128, nblk, 16])
            p.step("vector", lambda e: e.tensor_scalar(Bi, Xi, ABS_MASK, None,
                                                       Alu.bitwise_and))
            p.step("vector", lambda e: e.tensor_reduce(sc3, B3, AX, Alu.max))
            vsync()
            p.step("vector", lambda e: e.tensor_scalar(sc, sc, 1.0 / 6.0, 1e-12,
                                                       Alu.mult, Alu.max))
            vsync()
            p.step("vector", lambda e: e.reciprocal(rc, sc))
            vsync()
            p.step("vector", lambda e: e.tensor_tensor(B3, B3, rcb, Alu.mult))
            p.step("vector", lambda e: e.tensor_scalar(Ci, Bi, HALF_DOWN, None,
                                                       Alu.add))
            p.step("vector", lambda e: e.tensor_scalar(Ci, Ci, MASK_HI, None,
                                                       Alu.bitwise_and))
            p.step("vector", lambda e: e.tensor_scalar(Mk, B, 1.0, None,
                                                       Alu.is_lt))
            p.step("vector", lambda e: e.tensor_scalar(B, B, 1.0, None, Alu.add))
            p.step("vector", lambda e: e.tensor_scalar(Bi, Bi, HALF_DOWN, None,
                                                       Alu.add))
            p.step("vector", lambda e: e.tensor_scalar(Bi, Bi, MASK_HI, None,
                                                       Alu.bitwise_and))
            p.step("vector", lambda e: e.tensor_scalar(B, B, -1.0, None, Alu.add))
            p.step("vector", lambda e: e.select(C, Mk, B, C))
            p.step("vector", lambda e: e.tensor_tensor(C3, C3, scb, Alu.mult))
            p.step("vector", lambda e: e.scalar_tensor_tensor(
                Ci, Xi, SMK[:, 0:1], Ci, Alu.bitwise_and, Alu.bitwise_or))
            if QH is not None:
                p.step("vector", lambda e: e.tensor_copy(QH, C))
                p.step("vector", lambda e: e.tensor_tensor(QL, C, QH,
                                                           Alu.subtract))

        def layer_norm(xg):
            T = TN
            M = [T[:, 1:2], T[:, 4:5], T[:, 7:8], T[:, 10:11]]
            C = [T[:, 2:3], T[:, 5:6], T[:, 8:9], T[:, 11:12]]
            p.step("vector", lambda e, x=xg: e.bn_stats(T[:, 0:6], x[:, 0:512]))
            p.step("vector", lambda e, x=xg: e.bn_stats(T[:, 6:12],
                                                        x[:, 512:1024]))
            vsync()
            p.step("vector", lambda e: e.tensor_tensor(T[:, 16:17], M[0], M[1],
                                                       Alu.add))
            p.step("vector", lambda e: e.tensor_tensor(T[:, 17:18], M[2], M[3],
                                                       Alu.add))
            p.step("vector", lambda e: e.tensor_tensor(T[:, 18:19], M[0], M[0],
                                                       Alu.mult))
            p.step("vector", lambda e: e.tensor_tensor(T[:, 19:20], M[1], M[1],
                                                       Alu.mult))
            p.step("vector", lambda e: e.tensor_tensor(T[:, 20:21], M[2], M[2],
                                                       Alu.mult))
            p.step("vector", lambda e: e.tensor_tensor(T[:, 21:22], M[3], M[3],
                                                       Alu.mult))
            p.step("vector", lambda e: e.tensor_tensor(T[:, 22:23], C[0], C[1],
                                                       Alu.add))
            p.step("vector", lambda e: e.tensor_tensor(T[:, 23:24], C[2], C[3],
                                                       Alu.add))
            vsync()
            p.step("vector", lambda e: e.tensor_tensor(
                T[:, 12:13], T[:, 16:17], T[:, 17:18], Alu.add))
            p.step("vector", lambda e: e.tensor_tensor(
                T[:, 13:14], T[:, 22:23], T[:, 23:24], Alu.add))
            p.step("vector", lambda e: e.tensor_tensor(
                T[:, 16:17], T[:, 18:19], T[:, 19:20], Alu.add))
            p.step("vector", lambda e: e.tensor_tensor(
                T[:, 17:18], T[:, 20:21], T[:, 21:22], Alu.add))
            vsync()
            p.step("vector", lambda e: e.tensor_scalar(
                T[:, 12:13], T[:, 12:13], 0.25, None, Alu.mult))
            p.step("vector", lambda e: e.tensor_tensor(
                T[:, 14:15], T[:, 16:17], T[:, 17:18], Alu.add))
            vsync()
            p.step("vector", lambda e: e.tensor_scalar(
                T[:, 16:17], T[:, 14:15], 256.0, None, Alu.mult))
            p.step("vector", lambda e: e.tensor_tensor(
                T[:, 17:18], T[:, 12:13], T[:, 12:13], Alu.mult))
            vsync()
            p.step("vector", lambda e: e.tensor_tensor(
                T[:, 16:17], T[:, 16:17], T[:, 13:14], Alu.add))
            vsync()
            p.step("vector", lambda e: e.tensor_scalar(
                T[:, 15:16], T[:, 16:17], 1.0 / 1024.0, None, Alu.mult))
            vsync()
            p.step("vector", lambda e: e.tensor_tensor(
                T[:, 15:16], T[:, 15:16], T[:, 17:18], Alu.subtract))
            vsync()
            p.step("vector", lambda e: e.tensor_scalar(
                T[:, 14:15], T[:, 15:16], EPS, None, Alu.add), inc="sV")
            p.wait("scalar", "sV")
            p.step("scalar", lambda e: e.activation(
                T[:, 15:16], T[:, 14:15], Act.Sqrt), inc="sC")
            p.wait("vector", "sC")
            p.step("vector", lambda e: e.reciprocal(T[:, 8:9], T[:, 15:16]))
            vsync()
            p.step("vector", lambda e: e.tensor_tensor(
                T[:, 9:10], T[:, 8:9], T[:, 8:9], Alu.mult))
            vsync()
            p.step("vector", lambda e: e.tensor_tensor(
                T[:, 9:10], T[:, 9:10], T[:, 14:15], Alu.mult))
            vsync()
            p.step("vector", lambda e: e.tensor_scalar(
                T[:, 9:10], T[:, 9:10], -0.5, 1.5, Alu.mult, Alu.add))
            vsync()
            p.step("vector", lambda e: e.tensor_tensor(
                T[:, 10:11], T[:, 9:10], T[:, 8:9], Alu.mult))
            p.step("vector", lambda e, x=xg: e.tensor_scalar(
                x, x, T[:, 12:13], None, Alu.subtract))
            p.step("vector", lambda e, x=xg: e.tensor_scalar(
                x, x, T[:, 10:11], None, Alu.mult))

        XSA = arf(O_XG, 1024)
        XSB = arf(O_XG + 1024, 1024)
        XNO = {0: (O_XNH, O_XNL), 1: (12288, 14336),
               2: (O_XNH, O_XNL), 3: (29696, 31744)}
        stats_mark = {}
        xn_ready = {}
        prep_state = {"xsb_free": 0, "ap_mark": 0, "p1f": None,
                      "slot_free": {0: ("sP", 0), 1: ("sP", 0)},
                      "ap_fence": {0: 0, 1: 0}}
        T = TN

        def ln_stats_g(tau, g):
            rows0 = tau * TT + g * 128
            p.wait("gpsimd", "sV")
            if g == 0:
                p.wait("gpsimd", "sP", prep_state["xsb_free"])
            p.dma("gpsimd", XSA, x_h[rows0:rows0 + 128, :], "qG")
            fx = qfence("gpsimd")
            p.wait("vector", "fG", fx)
            b = g * 12
            p.step("vector", lambda e, bb=b: e.bn_stats(
                T[:, bb:bb + 6], XSA[:, 0:512]))
            p.step("vector", lambda e, bb=b: e.bn_stats(
                T[:, bb + 6:bb + 12], XSA[:, 512:1024]), inc="sV")
            stats_mark[(tau, g)] = p.n("sV")

        def _tt(d, a, b, op):
            p.step("vector", lambda e, dd=d, aa=a, bb=b, oo=op: e.tensor_tensor(
                T[:, dd:dd + 1], T[:, aa:aa + 1], T[:, bb:bb + 1], oo))

        def _ts(d, a, s1, s2, op1, op2=None):
            if op2 is None:
                p.step("vector", lambda e, dd=d, aa=a: e.tensor_scalar(
                    T[:, dd:dd + 1], T[:, aa:aa + 1], s1, s2, op1))
            else:
                p.step("vector", lambda e, dd=d, aa=a: e.tensor_scalar(
                    T[:, dd:dd + 1], T[:, aa:aa + 1], s1, s2, op1, op2))

        def ln_agg4():
            vsync()
            for g in range(NG):
                s, u = g * 12, 48 + 8 * g
                _tt(u + 0, s + 1, s + 4, Alu.add)
                _tt(u + 1, s + 7, s + 10, Alu.add)
                _tt(u + 2, s + 1, s + 1, Alu.mult)
                _tt(u + 3, s + 4, s + 4, Alu.mult)
                _tt(u + 4, s + 7, s + 7, Alu.mult)
                _tt(u + 5, s + 10, s + 10, Alu.mult)
                _tt(u + 6, s + 2, s + 5, Alu.add)
                _tt(u + 7, s + 8, s + 11, Alu.add)
            vsync()
            for g in range(NG):
                u = 48 + 8 * g
                _tt(80 + g, u + 0, u + 1, Alu.add)
                _tt(u + 6, u + 6, u + 7, Alu.add)
                _tt(u + 0, u + 2, u + 3, Alu.add)
                _tt(u + 1, u + 4, u + 5, Alu.add)
            vsync()
            for g in range(NG):
                u = 48 + 8 * g
                _ts(80 + g, 80 + g, 0.25, None, Alu.mult, None)
                _tt(u + 2, u + 0, u + 1, Alu.add)
            vsync()
            for g in range(NG):
                u = 48 + 8 * g
                _ts(u + 3, u + 2, 256.0, None, Alu.mult, None)
                _tt(u + 4, 80 + g, 80 + g, Alu.mult)
            vsync()
            for g in range(NG):
                u = 48 + 8 * g
                _tt(u + 3, u + 3, u + 6, Alu.add)
            vsync()
            for g in range(NG):
                u = 48 + 8 * g
                _ts(u + 5, u + 3, 1.0 / 1024.0, None, Alu.mult, None)
            vsync()
            for g in range(NG):
                u = 48 + 8 * g
                _tt(u + 5, u + 5, u + 4, Alu.subtract)
            vsync()
            for g in range(NG):
                u = 48 + 8 * g
                _ts(88 + g, u + 5, EPS, None, Alu.add, None)
            p.step("vector", lambda e: e.engine_nop(), inc="sV")
            p.wait("scalar", "sV")
            p.step("scalar", lambda e: e.activation(
                T[:, 92:96], T[:, 88:92], Act.Sqrt), inc="sC")
            p.wait("vector", "sC")
            p.step("vector", lambda e: e.reciprocal(T[:, 84:88], T[:, 92:96]))
            vsync()
            p.step("vector", lambda e: e.tensor_tensor(
                T[:, 48:52], T[:, 84:88], T[:, 84:88], Alu.mult))
            vsync()
            p.step("vector", lambda e: e.tensor_tensor(
                T[:, 48:52], T[:, 48:52], T[:, 88:92], Alu.mult))
            vsync()
            p.step("vector", lambda e: e.tensor_scalar(
                T[:, 48:52], T[:, 48:52], -0.5, 1.5, Alu.mult, Alu.add))
            vsync()
            p.step("vector", lambda e: e.tensor_tensor(
                T[:, 84:88], T[:, 48:52], T[:, 84:88], Alu.mult))
            vsync()

        def ln_apply_load(tau, g):
            slot = XSB if g % 2 == 0 else XSA
            rows0 = tau * TT + g * 128
            s, v = prep_state["slot_free"][g % 2]
            p.wait("gpsimd", s, v)
            p.dma("gpsimd", slot, x_h[rows0:rows0 + 128, :], "qG")
            prep_state["ap_fence"][g % 2] = qfence("gpsimd")

        def ln_apply_v(tau, g):
            slot = XSB if g % 2 == 0 else XSA
            p.wait("vector", "fG", prep_state["ap_fence"][g % 2])
            p.step("vector", lambda e, x=slot, gg=g: e.tensor_scalar(
                x, x, T[:, 80 + gg:81 + gg], None, Alu.subtract))
            p.step("vector", lambda e, x=slot, gg=g: e.tensor_scalar(
                x, x, T[:, 84 + gg:85 + gg], None, Alu.mult), inc="sV")
            prep_state["ap_mark"] = p.n("sV")
            if tau == 0 and g == 0:
                dbg("xn0", slot)

        def ln_apply_pe(tau, g):
            slot = XSB if g % 2 == 0 else XSA
            b0 = (g % 2) * 2
            p.wait("tensor", "sV", prep_state["ap_mark"])
            bank_wait("tensor", b0)
            bank_wait("tensor", b0 + 1)
            for k in range(NK):
                bb = b0 + k // 4
                cc = (k % 4) * 128
                p.step("tensor", lambda e, b=bb, c=cc, kk=k, x=slot: e.transpose(
                    pb[b][:, c:c + 128], x[:, kk * 128:(kk + 1) * 128],
                    IDF[:]), inc="sP" if k == NK - 1 else None)
            prep_state["xsb_free"] = p.n("sP")
            prep_state["slot_free"][g % 2] = ("sP", p.n("sP"))
            trm = p.n("sP")
            oh, ol = XNO[tau]
            p.wait("scalar", "sP", trm)
            if prep_state["p1f"] is not None and tau == 1:
                p.wait("scalar", "fR", prep_state["p1f"])
                p.wait("vector", "fR", prep_state["p1f"])
            if tau == 3 and g == 0:
                p.wait("scalar", "fR", p2_state["fence"])
                p.wait("vector", "fR", p2_state["fence"])
            for k in range(NK):
                bb = b0 + k // 4
                cc = (k % 4) * 128
                xnh = arb(oh + k * 256, 512)
                p.step("scalar", lambda e, o=xnh, b=bb, c=cc, gg=g:
                       e.activation(o[:, gg * 128:(gg + 1) * 128],
                                    pb[b][:, c:c + 128], Act.Copy),
                       inc="sC" if k == NK - 1 else None)
            him = p.n("sC")
            p.wait("vector", "sC", him)
            for k in range(NK):
                bb = b0 + k // 4
                cc = (k % 4) * 128
                xnh = arb(oh + k * 256, 512)
                xnl = arb(ol + k * 256, 512)
                p.step("vector", lambda e, o=xnl, b=bb, c=cc, gg=g, hh=xnh:
                       e.tensor_tensor(o[:, gg * 128:(gg + 1) * 128],
                                       pb[b][:, c:c + 128],
                                       hh[:, gg * 128:(gg + 1) * 128],
                                       Alu.subtract),
                       inc="sV" if k == NK - 1 else None)
            em = p.n("sV")
            bank_free[b0] = ("sV", em)
            bank_free[b0 + 1] = ("sV", em)
            if g == NG - 1:
                xn_ready[tau] = em

        def prep_step(tau, s):
            if s < NG:
                ln_stats_g(tau, s)
                if s == NG - 1:
                    prep_state["slot_free"][1] = ("sV", stats_mark[(tau, s)])
                    ln_agg4()
                    ln_apply_load(tau, 0)
            elif s == NG:
                ln_apply_v(tau, 0)
                ln_apply_load(tau, 1)
            elif s < NK - 1:
                g = s - NG
                ln_apply_pe(tau, g - 1)
                ln_apply_v(tau, g)
                ln_apply_load(tau, g + 1)
            else:
                ln_apply_pe(tau, 2)
                ln_apply_v(tau, 3)
                ln_apply_pe(tau, 3)

        PB1 = 8192
        p1_slot_fence = {}
        p1_chunks = [(wz_h, wzh_s, wzl_s, k) for k in range(NK)] +                     [(wh_h, whh_s, whl_s, k) for k in range(NK)]
        for j, (w_in, s_hi, s_lo, k) in enumerate(p1_chunks):
            slot = j % 4
            a_in = arf(PB1 + slot * 2048, 1024)
            a_hi = arb(PB1 + slot * 2048 + 1024, 1024)
            a_lo = arb(PB1 + slot * 2048 + 1536, 1024)
            if j >= 4:
                p.wait("sync", "fR", p1_slot_fence[slot])
            p.dma("sync", a_in, w_in[k * 128:(k + 1) * 128, :], "qS")
            fv = qfence("sync")
            p.wait("vector", "fS", fv)
            p.step("vector", lambda e, o=a_hi, i=a_in: e.tensor_copy(o, i))
            p.step("vector", lambda e, o=a_lo, i=a_in, h=a_hi:
                   e.tensor_tensor(o, i, h, Alu.subtract), inc="sV")
            p.wait("scalar", "sV")
            p.dma("scalar", s_hi[k * 128:(k + 1) * 128, :], a_hi, "qR")
            p.dma("scalar", s_lo[k * 128:(k + 1) * 128, :], a_lo, "qR")
            p1_slot_fence[slot] = qfence("scalar")
            if j % 2 == 1 and j // 2 < 8:
                prep_step(0, j // 2)
        p1_out_f = qfence("scalar")
        p.wait("sync", "fR", p1_out_f)
        prep_state["p1f"] = p1_out_f

        def wz_piece(roll, idx):
            o = O_WZ + (roll * 4 + idx) * 512
            return arb(o, 1024).rearrange("p (k j) -> p k j", j=128)

        mixer_scratch = [wzh_s, wzl_s, whh_s, whl_s]
        wz_piece_done = {}
        wz_buf_free = {}
        p2_jobs = [("d", ck) for ck in range(8)] + [("u", u) for u in range(6)]
        p2_keys = [(0, m) for m in range(NK)] + [(1, m) for m in range(NK - 1)]
        p2_state = {"fence": p1_out_f, "adone": {}}

        def p2_stageA(j):
            kind, arg = p2_jobs[j]
            w = 1024 if kind == "u" else 384
            a_in = arf(P2A, w)
            p.wait("gpsimd", "sV")
            if kind == "u":
                p.dma("gpsimd", a_in, ws1_h[arg * 128:(arg + 1) * 128, :], "qG")
            else:
                p.dma("gpsimd", a_in, ws2_h[arg * 128:(arg + 1) * 128, :], "qG")
            fp2 = qfence("gpsimd")
            p.wait("vector", "fG", fp2)
            quant(a_in, P2B, P2C, P2M, w // 16, w, arb(P2E, w), arb(P2F, w))
            p.step("vector", lambda e: e.engine_nop(), inc="sV")
            p2_state["adone"][j] = p.n("sV")

        def p2_stageB(j):
            kind, arg = p2_jobs[j]
            p.wait("tensor", "sV", p2_state["adone"][j])
            if kind == "u":
                bank_wait("tensor", 0)
                bank_wait("tensor", 1)
                for hl, so in ((0, P2E), (1, P2F)):
                    for k in range(NK):
                        p.step("tensor", lambda e, bb=hl, oo=so, kk=k:
                               e.transpose(
                            pb[bb][:].bitcast(BF16)[:, kk * 128:(kk + 1) * 128],
                            arb(oo, 1024)[:, kk * 128:(kk + 1) * 128], IDB[:]),
                            inc="sP" if (hl == 1 and k == NK - 1) else None)
                trc = p.n("sP")
                p.wait("vector", "sP", trc)
                if p2_state["fence"] is not None:
                    p.wait("vector", "fR", p2_state["fence"])
                p.step("vector", lambda e: e.tensor_copy(
                    arb(8192, 1024), pb[0][:].bitcast(BF16)[:, 0:1024]))
                p.step("vector", lambda e: e.tensor_copy(
                    arb(8704, 1024), pb[1][:].bitcast(BF16)[:, 0:1024]),
                    inc="sV")
                evc = p.n("sV")
                bank_free[0] = ("sV", evc)
                bank_free[1] = ("sV", evc)
                p.wait("scalar", "sV", evc)
                agt = ag1a_in if arg < 3 else ag1b_in
                for hl, eo in ((0, 8192), (1, 8704)):
                    col = (2 * (arg % 3) + hl) * 128
                    p.dma("scalar",
                          agt[:, col:col + 128].rearrange(
                              "(k p) j -> p k j", p=128),
                          arb(eo, 1024).rearrange(
                              "p (k j) -> p k j", j=128), "qR")
            else:
                bank_wait("tensor", 0)
                for hl, so in ((0, P2E), (1, P2F)):
                    for lb in range(3):
                        col = (hl * 3 + lb) * 128
                        p.step("tensor", lambda e, cc=col, oo=so, ll=lb:
                               e.transpose(
                            pb[0][:].bitcast(BF16)[:, cc:cc + 128],
                            arb(oo, 384)[:, ll * 128:(ll + 1) * 128], IDB[:]),
                            inc="sP" if (hl == 1 and lb == 2) else None)
                trc = p.n("sP")
                p.wait("vector", "sP", trc)
                if p2_state["fence"] is not None:
                    p.wait("vector", "fR", p2_state["fence"])
                p.step("vector", lambda e: e.tensor_copy(
                    arb(8192, 768), pb[0][:].bitcast(BF16)[:, 0:768]),
                    inc="sV")
                evc = p.n("sV")
                bank_free[0] = ("sV", evc)
                p.wait("scalar", "sV", evc)
                for hl in (0, 1):
                    for lb in range(3):
                        p.dma("scalar",
                              ag2_in[lb * 128:(lb + 1) * 128,
                                     hl * 1024 + arg * 128:
                                     hl * 1024 + (arg + 1) * 128],
                              arb(8192 + (hl * 3 + lb) * 64, 128), "qR")
            p2_state["fence"] = qfence("scalar")

        def p2_emit(key):
            i = p2_keys.index(key)
            if i > 0 and i - 1 < len(p2_jobs):
                p2_stageB(i - 1)
                trig = {7: (ag2_in, ag2_out), 10: (ag1a_in, ag1a_out),
                        13: (ag1b_in, ag1b_out)}.get(i - 1)
                if trig is not None:
                    p.wait("gpsimd", "fR", p2_state["fence"])
                    p.step("gpsimd", lambda e, a=trig[0], b=trig[1]:
                           e.collective_compute("AllGather", Alu.bypass,
                                                replica_groups=RG,
                                                ins=[a], outs=[b]), inc="sG")
            if i < len(p2_jobs):
                p2_stageA(i)
        h_written = {}

        for tau in range(NT):
            if tau == 0:
                dbg("xnh", arb(O_XNH, 4096))
                dbg("xnl", arb(O_XNL, 4096))

            def wz_fetch(tau_, m_):
                rollm_ = (tau_ * NK + m_) % 2
                for idx4, scr in enumerate(mixer_scratch):
                    key = (rollm_, idx4)
                    if key in wz_buf_free:
                        s, v = wz_buf_free[key]
                        p.wait("sync", s, v)
                    dst = wz_piece(rollm_, idx4)
                    srcw = scr.rearrange("(k q) (mm j) -> q k mm j",
                                         q=128, j=128)[:, :, m_, :]
                    p.dma("sync", dst, srcw, "qS")
                wz_piece_done[(tau_, m_)] = qfence("sync")
            if tau == 0:
                wz_fetch(0, 0)
            for m in range(NK):
                rollm = (tau * NK + m) % 2
                if m < NK - 1:
                    wz_fetch(tau, m + 1)
                elif tau < NT - 1:
                    wz_fetch(tau + 1, 0)
                z_done = h_done = None
                for mat in range(2):
                    bank = 4 + (2 * m + mat) % 4
                    bank_wait("tensor", bank)
                    p.wait("tensor", "fS", wz_piece_done[(tau, m)])
                    if m == 0 and mat == 0:
                        p.wait("tensor", "sV", xn_ready[tau])
                    hi_p = wz_piece(rollm, 2 * mat + 0)
                    lo_p = wz_piece(rollm, 2 * mat + 1)
                    xno_h, xno_l = XNO[tau]
                    for k in range(NK):
                        xnh = arb(xno_h + k * 256, 512)
                        xnl = arb(xno_l + k * 256, 512)
                        p.step("tensor", lambda e, b=bank, hp=hi_p, kk=k, r=xnh,
                               s0=(k == 0): e.matmul(
                            pb[b][:, 0:512], hp[:, kk, :], r,
                            start=s0, stop=False))
                        p.step("tensor", lambda e, b=bank, hp=hi_p, kk=k, r=xnl:
                               e.matmul(pb[b][:, 0:512], hp[:, kk, :], r,
                                        start=False, stop=False))
                        p.step("tensor", lambda e, b=bank, lp=lo_p, kk=k, r=xnh,
                               s1=(k == NK - 1): e.matmul(
                            pb[b][:, 0:512], lp[:, kk, :], r,
                            start=False, stop=s1),
                            inc="sP" if k == NK - 1 else None)
                    mm_done = p.n("sP")
                    p.wait("scalar", "sP", mm_done)
                    if mat == 0:
                        zb = arf(O_ZB + (m % 2) * 512, 512)
                        acol0 = AA[:, tau * 4096 + m * 512:
                                   tau * 4096 + (m + 1) * 512]
                        p.step("scalar", lambda e, o=zb, b=bank, mm=m:
                               e.activation(o, pb[b][:, 0:512], Act.Sigmoid,
                                            bias=BZ[:, mm:mm + 1]))
                        p.step("scalar", lambda e, o=acol0, b=bank, mm=m:
                               e.activation(o, pb[b][:, 0:512], Act.Sigmoid,
                                            bias=NBZ[:, mm:mm + 1],
                                            scale=-1.0), inc="sC")
                        z_done = p.n("sC")
                    else:
                        hb = arf(O_HT + (m % 2) * 512, 512)
                        p.step("scalar", lambda e, o=hb, b=bank, mm=m: e.add(
                            o, pb[b][:, 0:512], BH[:, mm:mm + 1]), inc="sC")
                        h_done = p.n("sC")
                    bank_free[bank] = ("sC", p.n("sC"))
                for idx4 in range(4):
                    wz_buf_free[(rollm, idx4)] = ("sP", mm_done)
                p.wait("vector", "sC", h_done)
                zb = arf(O_ZB + (m % 2) * 512, 512)
                hb = arf(O_HT + (m % 2) * 512, 512)
                acol = AA[:, tau * 4096 + m * 512: tau * 4096 + (m + 1) * 512]
                hcol = arf(O_H + tau * 4096 + m * 512, 512)
                bb = arf(O_BB, 512)
                cu = arf(O_CU, 512)
                if tau == 0 and m == 0:
                    dbg("z0", zb, eng="scalar")
                    dbg("ht0", hb, eng="scalar")
                if tau == 2 and m == 0:
                    p.wait("vector", "fR", p2_state["fence"])
                p.step("vector", lambda e, z=zb, h=hb, b=bb:
                       e.tensor_tensor(b, z, h, Alu.mult))
                if m > 0:
                    p.step("vector", lambda e, o=cu, mm=m - 1: e.tensor_copy(
                        AST[:, mm:mm + 1], o[:, 511:512]))
                p.step("vector", lambda e, o=hcol, a=acol, b=bb, mm=m:
                       e.tensor_tensor_scan(o, a, b, HST[:, mm:mm + 1],
                                            Alu.mult, Alu.add))
                p.step("vector", lambda e, o=cu, a=acol, mm=m:
                       e.tensor_tensor_scan(o, a, a, AST[:, mm:mm + 1],
                                            Alu.mult, Alu.bypass))
                p.step("vector", lambda e, o=hcol, mm=m: e.tensor_copy(
                    HST[:, mm:mm + 1], o[:, 511:512]), inc="sV")
                if (tau, m) in p2_keys:
                    p2_emit((tau, m))
                if tau < NT - 1:
                    prep_step(tau + 1, m)
            vsync()
            p.step("vector", lambda e: e.tensor_copy(
                AST[:, 7:8], arf(O_CU + 511, 1)), inc="sV")
            h_written[tau] = p.n("sV")

            if False:
                for kind, arg in p2_sched[tau]:
                    if kind == "u":
                        u = arg
                        a_in = arf(P2A, 1024)
                        p.wait("sync", "sV")
                        p.dma("sync", a_in, ws1_h[u * 128:(u + 1) * 128, :], "qS")
                        fp2 = qfence("sync")
                        p.wait("vector", "fS", fp2)
                        quant(a_in, P2B, P2C, P2M, 64, 1024,
                              arb(P2E, 1024), arb(P2F, 1024))
                        p.step("vector", lambda e: e.engine_nop(), inc="sV")
                        p.wait("scalar", "sV")
                        p.dma("scalar", ag1_in[u * 128:(u + 1) * 128, 0:1024],
                              arb(P2E, 1024), "qR")
                        p.dma("scalar", ag1_in[u * 128:(u + 1) * 128, 1024:2048],
                              arb(P2F, 1024), "qR")
                    else:
                        c0, w = arg
                        a_in = arf(P2A, w)
                        p.wait("sync", "sV")
                        p.dma("sync", a_in, ws2_h[:, c0:c0 + w], "qS")
                        fp2 = qfence("sync")
                        p.wait("vector", "fS", fp2)
                        quant(a_in, P2B, P2C, P2M, w // 16, w,
                              arb(P2E, w), arb(P2F, w))
                        p.step("vector", lambda e: e.engine_nop(), inc="sV")
                        p.wait("scalar", "sV")
                        p.dma("scalar", ag2_in[:, c0:c0 + w], arb(P2E, w), "qR")
                        p.dma("scalar", ag2_in[:, DI + c0:DI + c0 + w],
                              arb(P2F, w), "qR")
        dbg("a_all", AA)
        dbg("h_allM", arf(O_H, 16384))
        dbg("ast", AST[:])
        dbg("hst", HST[:])
        agw_done = 3

        p.wait("sync", "sP", prep_state["xsb_free"])
        p.dma("sync", arf(O_D, 1024), ldt_h, "qS")
        fld = qfence("sync")
        p.wait("scalar", "fS", fld)
        p.step("scalar", lambda e: e.activation(
            arf(O_D, 1024), arf(O_D, 1024), Act.Exp), inc="sC")
        d_ready = p.n("sC")

        p.wait("tensor", "sV", h_written[NT - 1])
        bank_wait("tensor", 4)
        p.step("tensor", lambda e: e.transpose(
            pb[4][0:8, 0:128], AST[:], IDF[:]))
        p.step("tensor", lambda e: e.transpose(
            pb[4][0:8, 128:256], HST[:], IDF[:]), inc="sP")
        p.wait("vector", "sP")
        p.step("vector", lambda e: e.tensor_copy(
            AR_t[0:8, O_GAT:O_GAT + 256], pb[4][0:8, 0:256]), inc="sV")
        bank_free[4] = ("sV", p.n("sV"))
        p.wait("gpsimd", "sV")
        p.dma("gpsimd", ccm_in[0:1, :].rearrange("a (c q) -> (a c) q", q=128),
              AR_t[0:8, O_GAT:O_GAT + 128], "qG")
        p.dma("gpsimd", ccm_in[1:2, :].rearrange("a (c q) -> (a c) q", q=128),
              AR_t[0:8, O_GAT + 128:O_GAT + 256], "qG")
        p.wait("gpsimd", "qG")
        p.step("gpsimd", lambda e: e.collective_compute(
            "AllReduce" if False else "AllGather", Alu.bypass,
            replica_groups=RG, ins=[ccm_in], outs=[ccm_out]), inc="sG")
        p.wait("gpsimd", "sG")
        p.dma("gpsimd", AR_t[0:16, O_GAT:O_GAT + 1024], ccm_out, "qG")
        fgat = qfence("gpsimd")
        p.wait("tensor", "fG", fgat)
        bank_wait("tensor", 0)
        for c in range(8):
            p.step("tensor", lambda e, cc=c: e.transpose(
                pb[0][:, cc * 16:(cc + 1) * 16],
                AR_t[0:16, O_GAT + cc * 128:O_GAT + (cc + 1) * 128],
                IDF[0:16, 0:16]), inc="sP" if c == 7 else None)
        p.wait("vector", "sP")
        SEQ = arf(O_SEQ, 128)
        p.step("vector", lambda e: e.tensor_copy(SEQ, pb[0][:, 0:128]),
               inc="sV")
        bank_free[0] = ("sV", p.n("sV"))
        vsync()
        PRF = arf(O_PRF, 64)
        for c in range(8):
            s3 = SEQ[:, c * 16:(c + 1) * 16].rearrange(
                "p (r two) -> p r two", two=2)
            p.step("vector", lambda e, cc=c, a=s3: e.tensor_tensor_scan(
                PRF[:, cc * 8:(cc + 1) * 8], a[:, :, 0:1].squeeze(2),
                a[:, :, 1:2].squeeze(2), 0.0, Alu.mult, Alu.add))
        vsync()
        p.step("vector", lambda e: e.tensor_tensor(
            arf(O_GAT, 64), PRF, SEL[:], Alu.mult))
        vsync()
        p.step("vector", lambda e: e.tensor_reduce(
            FST[:].rearrange("p (c o) -> p c o", o=1),
            arf(O_GAT, 64).rearrange("p (c r) -> p c r", r=8), AX, Alu.add),
            inc="sV")
        vsync()
        dbg("seq", SEQ)
        dbg("prf", PRF)
        dbg("fst", FST[:])

        p.wait("vector", "sC", d_ready)

        xg_used = {}
        for tau in range(NT):
            for m in range(NK):
                fb = arf(O_F + (m % 2) * 512, 512)
                acol = AA[:, tau * 4096 + m * 512: tau * 4096 + (m + 1) * 512]
                hcol = arf(O_H + tau * 4096 + m * 512, 512)
                p.step("vector", lambda e, o=fb, a=acol, mm=m:
                       e.tensor_tensor_scan(o, a, a, FST[:, mm:mm + 1],
                                            Alu.mult, Alu.bypass))
                p.step("vector", lambda e, h=hcol, f=fb: e.tensor_tensor(
                    h, h, f, Alu.add))
                p.step("vector", lambda e, o=fb, mm=m: e.tensor_copy(
                    FST[:, mm:mm + 1], o[:, 511:512]), inc="sV")
                if tau == 0 and m == 0:
                    dbg("fb0", fb)
            hf_done = p.n("sV")

            p.wait("tensor", "sV", hf_done)
            for g in range(NG):
                idx = tau * NG + g
                slot = g % 2
                xx = arf(O_XXG + slot * 1024, 1024)
                if idx >= 2:
                    p.wait("gpsimd", "sV", xg_used[idx - 2])
                rows0 = tau * TT + g * 128
                p.dma("gpsimd", xx, x_h[rows0:rows0 + 128, :], "qG")
                xload = qfence("gpsimd")
                b0, b1 = 2 * (g % 2), 2 * (g % 2) + 1
                bank_wait("tensor", b0)
                bank_wait("tensor", b1)
                for k in range(NK):
                    b = b0 if k < 4 else b1
                    p.step("tensor", lambda e, bb=b, kk=k, gg=g, t=tau:
                           e.transpose(
                        pb[bb][:, (kk % 4) * 128:(kk % 4 + 1) * 128],
                        arf(O_H + t * 4096 + kk * 512 + gg * 128, 128),
                        IDF[:]), inc="sP" if k == NK - 1 else None)
                trg = p.n("sP")
                slot = g % 2
                htm = arf(O_HTM + slot * 1024, 1024)
                xx = arf(O_XXG + slot * 1024, 1024)
                ycol = AA[:, tau * 4096 + g * 1024: tau * 4096 + (g + 1) * 1024]
                p.wait("vector", "sP", trg)
                p.step("vector", lambda e, o=htm, bb=b1: e.tensor_copy(
                    o[:, 512:1024], pb[bb][:, 0:512]))
                p.step("vector", lambda e, o=htm, bb=b0: e.tensor_copy(
                    o[:, 0:512], pb[bb][:, 0:512]), inc="sV")
                if tau == 0 and g == 0:
                    dbg("htm0", htm)
                bank_free[b0] = ("sV", p.n("sV"))
                bank_free[b1] = ("sV", p.n("sV"))
                p.wait("vector", "fG", xload)
                p.step("vector", lambda e, y=ycol, h=htm: e.tensor_tensor(
                    y[:, 0:1], h[:, 1023:1024], h[:, 1:2], Alu.add))
                p.step("vector", lambda e, y=ycol, h=htm: e.tensor_tensor(
                    y[:, 1023:1024], h[:, 1022:1023], h[:, 0:1], Alu.add))
                p.step("vector", lambda e, y=ycol, h=htm: e.tensor_tensor(
                    y[:, 1:1023], h[:, 0:1022], h[:, 2:1024], Alu.add))
                p.step("vector", lambda e, y=ycol, h=htm:
                       e.scalar_tensor_tensor(y, h, -2.0, y, Alu.mult, Alu.add))
                p.step("vector", lambda e, y=ycol: e.tensor_tensor(
                    y, y, arf(O_D, 1024), Alu.mult))
                p.step("vector", lambda e, y=ycol, h=htm: e.tensor_tensor(
                    y, y, h, Alu.add))
                p.step("vector", lambda e, y=ycol, x=xx: e.tensor_tensor(
                    y, y, x, Alu.add), inc="sV")
                xg_used[idx] = p.n("sV")

        dbg("h_allX", arf(O_H, 16384))
        dbg("yprov", AA)
        if DEBUG:
            fdbg = qfence("scalar")
            p.wait("vector", "fR", fdbg)
            p.wait("sync", "fR", fdbg)

        def wg_piece_ap(slot, W):
            return arb(O_WG + slot * 2048, 8 * W).rearrange(
                "p (k j) -> p k j", j=W)

        hq_state = {"trc": 0, "evc": 0}

        def emit_hq_tr(pending):
            n0, W, qcnt = pending
            nic = W // 128
            npair = 2 * nic
            p.wait("tensor", "sV", qcnt)
            for pr in range(npair):
                hl, ic = pr // nic, pr % nic
                bank = 4 + pr // 2
                if pr % 2 == 0:
                    bank_wait("tensor", bank)
                for g in range(NG):
                    srct = arb(O_HQS + hl * 1024 + g * (W // 2), W)[
                        :, ic * 128:(ic + 1) * 128]
                    p.step("tensor", lambda e, b=bank, s=srct, gg=g,
                           c=(pr % 2) * 512: e.transpose(
                        pb[b][:].bitcast(BF16)[:, c + gg * 128:c + (gg + 1) * 128],
                        s, IDB[:]),
                        inc="sP" if (pr == npair - 1 and g == NG - 1) else None)
            trc = p.n("sP")
            hq_state["trc"] = trc
            p.wait("scalar", "sP", trc)
            for pr in range(npair):
                hl, ic = pr // nic, pr % nic
                bank = 4 + pr // 2
                ig = n0 // 128 + ic
                dsto = (O_HQTH if hl == 0 else O_HQTL) + ig * 256
                p.step("scalar", lambda e, o=arb(dsto, 512), b=bank,
                       c=(pr % 2) * 512: e.activation(
                    o, pb[b][:].bitcast(BF16)[:, c:c + 512], Act.Copy),
                    inc="sC")
            for j in range(4):
                bank_free[4 + j] = ("sC", p.n("sC"))
            hq_state["evc"] = p.n("sC")

        def f_apply_v(t1, g):
            rows0 = t1 * TT + g * 128
            p.wait("scalar", "sV")
            p.dma("scalar", XSB, x_h[rows0:rows0 + 128, :], "qR")
            fx = qfence("scalar")
            p.wait("vector", "fR", fx)
            p.step("vector", lambda e, gg=g: e.tensor_scalar(
                XSB, XSB, T[:, 80 + gg:81 + gg], None, Alu.subtract))
            p.step("vector", lambda e, gg=g: e.tensor_scalar(
                XSB, XSB, T[:, 84 + gg:85 + gg], None, Alu.mult), inc="sV")

        def f_prep_step(t1, s):
            if s < NG:
                ln_stats_g(t1, s)
                if s == NG - 1:
                    ln_agg4()
            else:
                g = s - NG
                f_apply_v(t1, g)
                quant(XSB, 4096, g * 1024, 5120, 64, 1024)
                p.step("vector", lambda e: e.engine_nop(), inc="sV")

        f_stash_ready = {}
        FPREP_SCHED = {1: 0, 3: 1, 5: 2, 7: 3, 9: 4, 12: 5, 15: 6, 18: 7}

        p.wait("sync", "sG", agw_done)
        if DEBUG:
            p.wait("scalar", "sG", agw_done)
            p.dma("scalar", dbg_tensors["agin1"][:, 0:768], ag1a_in, "qR")
            p.dma("scalar", dbg_tensors["agin1"][:, 768:1536], ag1b_in, "qR")
            p.dma("scalar", dbg_tensors["agout1"][:, 0:768], ag1a_out, "qR")
            p.dma("scalar", dbg_tensors["agout1"][:, 768:1536], ag1b_out, "qR")
            p.dma("scalar", dbg_tensors["agin2"], ag2_in, "qR")
            p.dma("scalar", dbg_tensors["agout2"], ag2_out, "qR")
        p.wait("sync", "sV")
        p.wait("scalar", "sV")

        fxg_free = {}
        wg_slot_free = {}
        store_done = {}
        for tau in range(NT):
            xq_ev = {}
            for g in range(NG):
                if tau == 0:
                    idx = tau * NG + g
                    slot = g % 2
                    xg = arf(O_FXG + slot * 1024, 1024)
                    if idx >= 2:
                        p.wait("scalar", "sP", fxg_free[idx - 2])
                    rows0 = tau * TT + g * 128
                    p.dma("scalar", xg, x_h[rows0:rows0 + 128, :], "qR")
                    fx = qfence("scalar")
                    p.wait("vector", "fR", fx)
                    layer_norm(xg)
                    quant(xg, O_S1, O_S2, O_S3, 64, 1024)
                    p.step("vector", lambda e, x=xg: e.tensor_copy(
                        x, arf(O_S2, 1024)), inc="sV")
                    p.wait("tensor", "sV", p.n("sV"))
                else:
                    xg = arf(g * 1024, 1024)
                    if g == 0:
                        p.wait("tensor", "sV", f_stash_ready[tau])
                if g > 0:
                    p.wait("tensor", "sV", xq_ev[g - 1])
                for k in range(NK):
                    if g == 0:
                        bank_wait("tensor", k)
                    p.step("tensor", lambda e, b=k, gg=g, x=xg: e.transpose(
                        pb[b][:, gg * 128:(gg + 1) * 128],
                        x[:, b * 128:(b + 1) * 128], IDF[:]),
                        inc="sP" if k == NK - 1 else None)
                fxg_free[tau * NG + g] = p.n("sP")
                p.wait("vector", "sP", fxg_free[tau * NG + g])
                for k in range(NK):
                    xqh = arb(O_XQH + k * 256 + g * 64, 128)
                    xql = arb(O_XQL + k * 256 + g * 64, 128)
                    p.step("vector", lambda e, o=xqh, b=k, gg=g: e.tensor_copy(
                        o, pb[b][:, gg * 128:(gg + 1) * 128]))
                    p.step("vector", lambda e, o=xql, b=k, gg=g, h=xqh:
                           e.tensor_tensor(o, pb[b][:, gg * 128:(gg + 1) * 128],
                                           h, Alu.subtract),
                           inc="sV" if k == NK - 1 else None)
                xq_ev[g] = p.n("sV")
                if g == NG - 1:
                    for k in range(NK):
                        bank_free[k] = ("sV", xq_ev[g])

            pending = None
            for ni, (n0, W) in enumerate(NSL):
                for mat in range(2):
                    sA, sB = 2 * mat, 2 * mat + 1
                    for sslot, hl in ((sA, 0), (sB, 1)):
                        if sslot in wg_slot_free:
                            p.wait("sync", "sP", wg_slot_free[sslot])
                        dst = wg_piece_ap(sslot, W)
                        for uu in range(W // 128):
                            ju = 22 * mat + n0 // 128 + uu
                            cj, lj = ju // 6, ju % 6
                            agt = ag1a_out if lj < 3 else ag1b_out
                            col = (2 * (lj % 3) + hl) * 128
                            src = agt[cj * D:(cj + 1) * D,
                                      col:col + 128].rearrange(
                                "(k q) j -> q k j", q=128)
                            p.dma("sync", dst[:, :, uu * 128:(uu + 1) * 128],
                                  src, "qS")
                    piece_f = qfence("sync")
                    if DEBUG and tau == 0 and ni == 0 and mat == 0:
                        p.wait("scalar", "fS", piece_f)
                        p.dma("scalar", dbg_tensors["piece0"],
                              arb(O_WG, 4096), "qR")
                    p.wait("tensor", "fS", piece_f)
                    hi_pc = wg_piece_ap(sA, W)
                    lo_pc = wg_piece_ap(sB, W)
                    g_stop = {}
                    for g in range(NG):
                        bank = 4 * mat + g
                        bank_wait("tensor", bank)
                        if ni == 0 and mat == 0:
                            p.wait("tensor", "sV", xq_ev[g])
                        for ph, (so, pc) in enumerate(
                                [(O_XQH, hi_pc), (O_XQL, hi_pc),
                                 (O_XQH, lo_pc)]):
                            for k in range(NK):
                                stat = arb(so + k * 256 + g * 64, 128)
                                p.step("tensor", lambda e, b=bank, st=stat,
                                       mv=pc, kk=k, ww=W,
                                       s0=(ph == 0 and k == 0),
                                       s1=(ph == 2 and k == NK - 1): e.matmul(
                                    pb[b][:, 0:ww], st, mv[:, kk, :],
                                    start=s0, stop=s1),
                                    inc="sP" if (ph == 2 and k == NK - 1)
                                    else None)
                        g_stop[g] = p.n("sP")
                    wg_slot_free[sA] = p.n("sP")
                    wg_slot_free[sB] = p.n("sP")
                    if mat == 0:
                        for g in range(NG):
                            gs = arf(O_GS + g * 512, W)
                            p.wait("scalar", "sP", g_stop[g])
                            p.step("scalar", lambda e, o=gs, b=g, ww=W:
                                   e.activation(o, pb[b][:, 0:ww], Act.Silu),
                                   inc="sC")
                            bank_free[g] = ("sC", p.n("sC"))
                    else:
                        for g in range(NG):
                            gs = arf(O_GS + g * 512, W)
                            h2 = arf(O_H2 + g * 512, W)
                            p.wait("vector", "sP", g_stop[g])
                            p.wait("vector", "sC")
                            p.step("vector", lambda e, o=h2, b=4 + g, s=gs,
                                   ww=W: e.tensor_tensor(
                                o, pb[b][:, 0:ww], s, Alu.mult),
                                inc="sV")
                            bank_free[4 + g] = ("sV", p.n("sV"))
                if pending is not None:
                    emit_hq_tr(pending)
                    pending = None
                p.wait("vector", "sP", hq_state["trc"])
                if W == 512:
                    quant(arf(O_H2, 2048), O_FXG, O_S1, O_S3, 128, 2048,
                          arb(O_HQS, 2048), arb(O_HQS + 1024, 2048))
                else:
                    for g in range(NG):
                        quant(arf(O_H2 + g * 512, W), O_FXG, O_S1, O_S3,
                              W // 16, W,
                              arb(O_HQS + g * (W // 2), W),
                              arb(O_HQS + 1024 + g * (W // 2), W))
                p.step("vector", lambda e: e.engine_nop(), inc="sV")
                pending = (n0, W, p.n("sV"))
            emit_hq_tr(pending)
            pending = None
            hqT_done = p.n("sC")

            wd_free = {}
            for i in range(NI):
                roll = i % 2
                if i >= 2:
                    p.wait("sync", "sP", wd_free[i - 2])
                if i == 0:
                    p.wait("sync", "sC", hqT_done)
                dsth = arb(O_WD + roll * 1024, 1024)
                dstl = arb(O_WD + roll * 1024 + 512, 1024)
                cc, lb = (i // 3, i % 3) if i < 18 else \
                    (6 + (i - 18) // 2, (i - 18) % 2)
                r0 = cc * 384 + lb * 128
                p.dma("sync", dsth, ag2_out[r0:r0 + 128, 0:1024], "qS")
                p.dma("sync", dstl, ag2_out[r0:r0 + 128, 1024:2048], "qS")
                dwd = qfence("sync")
                if DEBUG and tau == 0 and i == 0:
                    p.wait("scalar", "fS", dwd)
                    p.dma("scalar", dbg_tensors["wd0"], arb(O_WD, 2048), "qR")
                p.wait("tensor", "fS", dwd)
                if i == 0:
                    for b in range(8):
                        bank_wait("tensor", b)
                mh = dsth.rearrange("p (a j) -> p a j", a=2)
                ml = dstl.rearrange("p (a j) -> p a j", a=2)
                for g in range(NG):
                    sth = arb(O_HQTH + i * 256, 512)[:, g * 128:(g + 1) * 128]
                    stl = arb(O_HQTL + i * 256, 512)[:, g * 128:(g + 1) * 128]
                    for half in range(2):
                        b = 2 * g + half
                        last = (i == NI - 1)
                        p.step("tensor", lambda e, bb=b, s=sth, m=mh, hh=half,
                               s0=(i == 0): e.matmul(
                            pb[bb][:, 0:512], s, m[:, hh, :],
                            start=s0, stop=False))
                        p.step("tensor", lambda e, bb=b, s=sth, m=ml, hh=half:
                               e.matmul(pb[bb][:, 0:512], s, m[:, hh, :],
                                        start=False, stop=False))
                        p.step("tensor", lambda e, bb=b, s=stl, m=mh, hh=half,
                               s1=last: e.matmul(
                            pb[bb][:, 0:512], s, m[:, hh, :],
                            start=False, stop=s1),
                            inc="sP" if (g == NG - 1 and half == 1) else None)
                wd_free[i] = p.n("sP")
                if tau < NT - 1 and i in FPREP_SCHED:
                    f_prep_step(tau + 1, FPREP_SCHED[i])
                    if FPREP_SCHED[i] == 7:
                        f_stash_ready[tau + 1] = p.n("sV")
            down_done = p.n("sP")

            p.wait("vector", "sP", down_done)
            for g in range(NG):
                idx = tau * NG + g
                ybuf = arf(O_WG + (g % 2) * 1024, 1024)
                if idx >= 2:
                    p.wait("vector", "fR", store_done[idx - 2])
                ycol = AA[:, tau * 4096 + g * 1024: tau * 4096 + (g + 1) * 1024]
                p.step("vector", lambda e, o=ybuf, b=2 * g, y=ycol:
                       e.tensor_tensor(o[:, 0:512], pb[b][:, 0:512],
                                       y[:, 0:512], Alu.add))
                p.step("vector", lambda e, o=ybuf, b=2 * g + 1, y=ycol:
                       e.tensor_tensor(o[:, 512:1024], pb[b][:, 0:512],
                                       y[:, 512:1024], Alu.add), inc="sV")
                bank_free[2 * g] = ("sV", p.n("sV"))
                bank_free[2 * g + 1] = ("sV", p.n("sV"))
                p.wait("scalar", "sV")
                rows0 = tau * TT + g * 128
                p.dma("scalar", y_h[rows0:rows0 + 128, :], ybuf, "qR")
                store_done[idx] = qfence("scalar")
            for sslot in range(4):
                wg_slot_free.pop(sslot, None)
            p.wait("sync", "fR", p.n("fR"))

        p.wait("scalar", "qR", p.n("qR"))

        block = es.enter_context(nc.Block())
        p.emit(block)

    return nc


_NC_CACHE = None


def kernel(L0_attn_x, L0_attn_Wz, L0_attn_bz, L0_attn_Wh, L0_attn_bh,
           L0_attn_log_D, L0_ffn_w_gate, L0_ffn_w_up, L0_ffn_w_down):
    global _NC_CACHE
    x = np.ascontiguousarray(np.asarray(L0_attn_x, dtype=np.float32)
                             ).reshape(16384, 1024)
    wz = np.ascontiguousarray(np.asarray(L0_attn_Wz, dtype=np.float32))
    wh = np.ascontiguousarray(np.asarray(L0_attn_Wh, dtype=np.float32))
    bz = np.asarray(L0_attn_bz, dtype=np.float32)
    bh = np.asarray(L0_attn_bh, dtype=np.float32)
    ld = np.asarray(L0_attn_log_D, dtype=np.float32)
    wg = np.ascontiguousarray(np.asarray(L0_ffn_w_gate, dtype=np.float32))
    wu = np.ascontiguousarray(np.asarray(L0_ffn_w_up, dtype=np.float32))
    wd = np.ascontiguousarray(np.asarray(L0_ffn_w_down, dtype=np.float32))

    bzp = np.ascontiguousarray(bz.reshape(8, 128).T)
    bhp = np.ascontiguousarray(bh.reshape(8, 128).T)
    ldt = np.ascontiguousarray(np.tile(ld[None, :], (128, 1)))
    idf = np.eye(128, dtype=np.float32)
    idb = np.eye(128, dtype=ml_dtypes.bfloat16)

    units = [wg[i * 128:(i + 1) * 128, :] for i in range(22)] + \
            [wu[i * 128:(i + 1) * 128, :] for i in range(22)] + \
            [np.zeros((128, 1024), np.float32)] * 4

    if _NC_CACHE is None:
        _NC_CACHE = build_nc()
    nc = _NC_CACHE

    in_maps = []
    for c in range(NCORES):
        onehot = np.zeros(8, np.float32)
        if c > 0:
            onehot[c - 1] = 1.0
        sel = np.ascontiguousarray(np.tile(onehot[None, :], (128, 8)))
        wsh1 = np.ascontiguousarray(
            np.concatenate(units[6 * c:6 * c + 6], axis=0))
        if c < 6:
            wsl = wd[:, c * 384:(c + 1) * 384]
        else:
            wsl = np.concatenate(
                [wd[:, 2304 + (c - 6) * 256: 2304 + (c - 5) * 256],
                 np.zeros((1024, 128), np.float32)], axis=1)
        wsh2 = np.ascontiguousarray(wsl)
        in_maps.append({
            "x": np.ascontiguousarray(x[c * TC:(c + 1) * TC, :]),
            "wz": wz, "wh": wh, "bzp": bzp, "bhp": bhp, "ldt": ldt,
            "wsh1": wsh1, "wsh2": wsh2, "idf": idf, "idb": idb, "sel": sel,
        })

    res = run_bass_kernel_spmd(nc, in_maps, core_ids=list(range(NCORES)))
    out = np.concatenate([res.results[c]["y"] for c in range(NCORES)], axis=0)
    return out.reshape(1, 16384, 1024)


# version 32
# speedup vs baseline: 1.0002x; 1.0002x over previous
import numpy as np
import ml_dtypes
import concourse.bass as bass
import concourse.mybir as mybir
from concourse.bass_utils import run_bass_kernel_spmd
from concourse.replica_groups import maybe_share_collective_output_space

F32 = mybir.dt.float32
BF16 = mybir.dt.bfloat16
I32 = mybir.dt.int32
I16 = mybir.dt.int16
Alu = mybir.AluOpType
Act = mybir.ActivationFunctionType
AX = mybir.AxisListType.X

NCORES = 8
TC = 2048
D = 1024
DI = 2816
NT = 4
TT = 512
NG = 4
NK = 8
NI = 22
EPS = 1e-5
MASK_HI = -4194304
HALF_DOWN = 0x1FFFFF
ABS_MASK = 0x7FFFFFFF
SIGN_MASK = -2147483648

NSL = [(0, 512), (512, 512), (1024, 512), (1536, 512), (2048, 512), (2560, 256)]
RG = [[0, 1, 2, 3, 4, 5, 6, 7]]

ARN = 34816
O_H = 0
O_XG = 16384
O_XNH = 18432
O_XNL = 20480
O_WZ = 22528
O_ZB = 26624
O_HT = 27648
O_BB = 28672
O_CU = 29184
O_P2 = 29696
P2A = O_P2
P2B = O_P2 + 1024
P2C = O_P2 + 2048
P2M = O_P2 + 3072
P2E = O_P2 + 3584
P2F = O_P2 + 4096
O_D = 16384
O_HTM = 17408
O_F = 19456
O_GAT = 20480
O_SEQ = 21504
O_PRF = 21632
O_XXG = 21760
O_FXG = 0
O_S1 = 2048
O_S2 = 3072
O_S3 = 4096
O_XQH = 5120
O_XQL = 7168
O_WG = 9216
O_HQS = 17408
O_HQTH = 19456
O_HQTL = 25088
O_WD = 30720
O_H2 = 30720
O_GS = 32768

ENGINES = ["sync", "scalar", "vector", "tensor", "gpsimd"]
DEBUG = False


class Plan:
    def __init__(self, nc):
        self.nc = nc
        self.steps = {e: [] for e in ENGINES}
        self.sems = {}
        self.cnt = {}

    def sem(self, name, es):
        h = es.enter_context(self.nc.semaphore(name))
        self.sems[name] = h
        self.cnt[name] = 0

    def step(self, eng, f, inc=None, n=1):
        if inc is not None:
            self.cnt[inc] += n
        self.steps[eng].append(("op", f, inc, n))

    def dma(self, eng, out, in_, inc, **kw):
        self.cnt[inc] += 16
        self.steps[eng].append(
            ("op", (lambda e, o=out, i=in_, k=dict(kw):
                    e.dma_start(out=o, in_=i, **k)), inc, 16))

    def wait(self, eng, sem, val=None):
        v = self.cnt[sem] if val is None else val
        self.steps[eng].append(("wait", sem, v, 0))

    def n(self, sem):
        return self.cnt[sem]

    def emit(self, block):
        for eng in ENGINES:
            steps = self.steps[eng]
            if not steps:
                continue
            deco = getattr(block, eng)

            def make(steps=steps):
                def body(e):
                    for kind, a, b, c in steps:
                        if kind == "wait":
                            e.wait_ge(self.sems[a], b)
                        else:
                            inst = a(e)
                            if b is not None and inst is not None:
                                inst.then_inc(self.sems[b], c)
                return body
            deco(make())


def build_nc():
    nc = bass.Bass("TRN2", target_bir_lowering=False, debug=False,
                   detect_race_conditions=False)

    x_h = nc.dram_tensor("x", [TC, D], F32, kind="ExternalInput").ap()
    wz_h = nc.dram_tensor("wz", [D, D], F32, kind="ExternalInput").ap()
    wh_h = nc.dram_tensor("wh", [D, D], F32, kind="ExternalInput").ap()
    bz_h = nc.dram_tensor("bzp", [128, 8], F32, kind="ExternalInput").ap()
    bh_h = nc.dram_tensor("bhp", [128, 8], F32, kind="ExternalInput").ap()
    ldt_h = nc.dram_tensor("ldt", [128, D], F32, kind="ExternalInput").ap()
    ws1_h = nc.dram_tensor("wsh1", [768, D], F32, kind="ExternalInput").ap()
    ws2_h = nc.dram_tensor("wsh2", [D, 384], F32, kind="ExternalInput").ap()
    idf_h = nc.dram_tensor("idf", [128, 128], F32, kind="ExternalInput").ap()
    idb_h = nc.dram_tensor("idb", [128, 128], BF16, kind="ExternalInput").ap()
    sel_h = nc.dram_tensor("sel", [128, 64], F32, kind="ExternalInput").ap()
    y_h = nc.dram_tensor("y", [TC, D], F32, kind="ExternalOutput").ap()

    wzh_s = nc.dram_tensor("wzh_s", [D, D], BF16, kind="Internal").ap()
    wzl_s = nc.dram_tensor("wzl_s", [D, D], BF16, kind="Internal").ap()
    whh_s = nc.dram_tensor("whh_s", [D, D], BF16, kind="Internal").ap()
    whl_s = nc.dram_tensor("whl_s", [D, D], BF16, kind="Internal").ap()
    sp1 = maybe_share_collective_output_space("AllGather", RG)
    spl = maybe_share_collective_output_space("AllGather", RG,
                                              prefer_local=True)
    ag1a_in = nc.dram_tensor("ag1a_in", [D, 768], BF16, kind="Internal").ap()
    ag1a_out = nc.dram_tensor("ag1a_out", [8 * D, 768], BF16, kind="Internal",
                              addr_space=sp1).ap()
    ag1b_in = nc.dram_tensor("ag1b_in", [D, 768], BF16, kind="Internal").ap()
    ag1b_out = nc.dram_tensor("ag1b_out", [8 * D, 768], BF16, kind="Internal",
                              addr_space=sp1).ap()
    ag2_in = nc.dram_tensor("ag2_in", [384, 2048], BF16, kind="Internal").ap()
    ag2_out = nc.dram_tensor("ag2_out", [3072, 2048], BF16, kind="Internal",
                             addr_space=sp1).ap()
    ccm_in = nc.dram_tensor("ccm_in", [2, D], F32, kind="Internal").ap()
    ccm_out = nc.dram_tensor("ccm_out", [16, D], F32, kind="Internal",
                             addr_space=spl).ap()

    dbg_tensors = {}

    def mkdbg(name, shape, dt=F32):
        dbg_tensors[name] = nc.dram_tensor("dbg_" + name, shape, dt,
                                           kind="ExternalOutput").ap()

    if DEBUG:
        mkdbg("xn0", [128, 1024])
        mkdbg("xnh", [128, 4096], BF16)
        mkdbg("xnl", [128, 4096], BF16)
        mkdbg("z0", [128, 512])
        mkdbg("ht0", [128, 512])
        mkdbg("a_all", [128, 16384])
        mkdbg("h_allM", [128, 16384])
        mkdbg("ast", [128, 8])
        mkdbg("hst", [128, 8])
        mkdbg("fst", [128, 8])
        mkdbg("seq", [128, 128])
        mkdbg("prf", [128, 64])
        mkdbg("h_allX", [128, 16384])
        mkdbg("htm0", [128, 1024])
        mkdbg("fb0", [128, 512])
        mkdbg("yprov", [128, 16384])
        mkdbg("xq0", [128, 1024])
        mkdbg("xqh", [128, 4096], BF16)
        mkdbg("gs0", [128, 2048])
        mkdbg("h20", [128, 2048])
        mkdbg("hqs", [128, 2048], BF16)
        mkdbg("hqth", [128, 11264], BF16)
        mkdbg("agin1", [1024, 1536], BF16)
        mkdbg("agout1", [8192, 1536], BF16)
        mkdbg("agin2", [384, 2048], BF16)
        mkdbg("agout2", [3072, 2048], BF16)
        mkdbg("piece0", [128, 4096], BF16)
        mkdbg("wd0", [128, 2048], BF16)

    from contextlib import ExitStack
    es = ExitStack()
    with es:
        AR_t = es.enter_context(nc.sbuf_tensor("AR", [128, ARN], F32))
        AA_t = es.enter_context(nc.sbuf_tensor("AA", [128, 16384], F32))
        IDF = es.enter_context(nc.sbuf_tensor("IDF", [128, 128], F32))
        IDB = es.enter_context(nc.sbuf_tensor("IDB", [128, 128], BF16))
        BZ = es.enter_context(nc.sbuf_tensor("BZ", [128, 8], F32))
        NBZ = es.enter_context(nc.sbuf_tensor("NBZ", [128, 8], F32))
        BH = es.enter_context(nc.sbuf_tensor("BH", [128, 8], F32))
        SEL = es.enter_context(nc.sbuf_tensor("SEL", [128, 64], F32))
        HST = es.enter_context(nc.sbuf_tensor("HST", [128, 8], F32))
        AST = es.enter_context(nc.sbuf_tensor("AST", [128, 8], F32))
        FST = es.enter_context(nc.sbuf_tensor("FST", [128, 8], F32))
        SC = es.enter_context(nc.sbuf_tensor("SC", [128, 128], F32))
        RC = es.enter_context(nc.sbuf_tensor("RC", [128, 128], F32))
        TN = es.enter_context(nc.sbuf_tensor("TN", [128, 96], F32))
        SMK = es.enter_context(nc.sbuf_tensor("SMK", [128, 1], I32))
        pb = [es.enter_context(nc.psum_tensor(f"pb{i}", [128, 512], F32))
              for i in range(8)]

        AR = AR_t[:]
        AA = AA_t[:]
        AR16 = AR.bitcast(BF16)
        ARI = AR.bitcast(I32)
        ARS = AR.bitcast(I16)

        def arf(o, n):
            return AR[:, o:o + n]

        def arb(o, n):
            return AR16[:, 2 * o:2 * o + n]

        def ari(o, n):
            return ARI[:, o:o + n]

        p = Plan(nc)
        for s in ["qS", "qR", "sP", "sV", "sC", "sG", "fS", "fR", "sY",
                  "qG", "fG"]:
            p.sem(s, es)

        def vsync():
            p.step("vector", lambda e: e.engine_nop(), inc="sY")
            p.wait("vector", "sY")

        QMAP = {"sync": ("qS", "fS"), "scalar": ("qR", "fR"),
                "gpsimd": ("qG", "fG")}

        def qfence(queue):
            qsem, fsem = QMAP[queue]
            p.wait(queue, qsem)
            p.step(queue, lambda e: e.nop(), inc=fsem)
            return p.n(fsem)

        def dbg(name, ap, eng="vector"):
            if not DEBUG or name not in dbg_tensors:
                return
            p.wait("scalar", "sV" if eng == "vector" else
                   ("sC" if eng == "scalar" else "sP"))
            p.dma("scalar", dbg_tensors[name], ap, "qR")

        bank_free = [None] * 8

        def bank_wait(eng, b):
            if bank_free[b] is not None:
                s, v = bank_free[b]
                p.wait(eng, s, v)

        p.dma("sync", IDF[:], idf_h, "qS")
        p.dma("sync", IDB[:], idb_h, "qS")
        p.dma("sync", BZ[:], bz_h, "qS")
        p.dma("sync", BH[:], bh_h, "qS")
        p.dma("sync", SEL[:], sel_h, "qS")
        consts_f = qfence("sync")
        p.wait("tensor", "fS", consts_f)
        p.wait("scalar", "fS", consts_f)
        p.wait("vector", "fS", consts_f)
        p.step("vector", lambda e: e.memset(HST[:], 0.0))
        p.step("vector", lambda e: e.tensor_scalar(
            NBZ[:], BZ[:], -1.0, None, Alu.mult))
        p.step("vector", lambda e: e.memset(AST[:], 1.0))
        p.step("vector", lambda e: e.memset(SMK[:], SIGN_MASK))

        def quant(X, oB, oC, oM, nblk, width, QH=None, QL=None):
            B = arf(oB, width)
            Bi = ari(oB, width)
            C = arf(oC, width)
            Ci = ari(oC, width)
            Mk = ARS[:, 2 * oM:2 * oM + width]
            Xi = X.bitcast(I32)
            B3 = B.rearrange("p (b j) -> p b j", j=16)
            C3 = C.rearrange("p (b j) -> p b j", j=16)
            sc = SC[:, 0:nblk]
            rc = RC[:, 0:nblk]
            sc3 = sc.rearrange("p (b o) -> p b o", o=1)
            rcb = rc.unsqueeze(2).broadcast_to([128, nblk, 16])
            scb = sc.unsqueeze(2).broadcast_to([128, nblk, 16])
            p.step("vector", lambda e: e.tensor_scalar(Bi, Xi, ABS_MASK, None,
                                                       Alu.bitwise_and))
            p.step("vector", lambda e: e.tensor_reduce(sc3, B3, AX, Alu.max))
            vsync()
            p.step("vector", lambda e: e.tensor_scalar(sc, sc, 1.0 / 6.0, 1e-12,
                                                       Alu.mult, Alu.max))
            vsync()
            p.step("vector", lambda e: e.reciprocal(rc, sc))
            vsync()
            p.step("vector", lambda e: e.tensor_tensor(B3, B3, rcb, Alu.mult))
            p.step("vector", lambda e: e.tensor_scalar(Ci, Bi, HALF_DOWN, None,
                                                       Alu.add))
            p.step("vector", lambda e: e.tensor_scalar(Ci, Ci, MASK_HI, None,
                                                       Alu.bitwise_and))
            p.step("vector", lambda e: e.tensor_scalar(Mk, B, 1.0, None,
                                                       Alu.is_lt))
            p.step("vector", lambda e: e.tensor_scalar(B, B, 1.0, None, Alu.add))
            p.step("vector", lambda e: e.tensor_scalar(Bi, Bi, HALF_DOWN, None,
                                                       Alu.add))
            p.step("vector", lambda e: e.tensor_scalar(Bi, Bi, MASK_HI, None,
                                                       Alu.bitwise_and))
            p.step("vector", lambda e: e.tensor_scalar(B, B, -1.0, None, Alu.add))
            p.step("vector", lambda e: e.select(C, Mk, B, C))
            p.step("vector", lambda e: e.tensor_tensor(C3, C3, scb, Alu.mult))
            p.step("vector", lambda e: e.scalar_tensor_tensor(
                Ci, Xi, SMK[:, 0:1], Ci, Alu.bitwise_and, Alu.bitwise_or))
            if QH is not None:
                p.step("vector", lambda e: e.tensor_copy(QH, C))
                p.step("vector", lambda e: e.tensor_tensor(QL, C, QH,
                                                           Alu.subtract))

        def layer_norm(xg):
            T = TN
            M = [T[:, 1:2], T[:, 4:5], T[:, 7:8], T[:, 10:11]]
            C = [T[:, 2:3], T[:, 5:6], T[:, 8:9], T[:, 11:12]]
            p.step("vector", lambda e, x=xg: e.bn_stats(T[:, 0:6], x[:, 0:512]))
            p.step("vector", lambda e, x=xg: e.bn_stats(T[:, 6:12],
                                                        x[:, 512:1024]))
            vsync()
            p.step("vector", lambda e: e.tensor_tensor(T[:, 16:17], M[0], M[1],
                                                       Alu.add))
            p.step("vector", lambda e: e.tensor_tensor(T[:, 17:18], M[2], M[3],
                                                       Alu.add))
            p.step("vector", lambda e: e.tensor_tensor(T[:, 18:19], M[0], M[0],
                                                       Alu.mult))
            p.step("vector", lambda e: e.tensor_tensor(T[:, 19:20], M[1], M[1],
                                                       Alu.mult))
            p.step("vector", lambda e: e.tensor_tensor(T[:, 20:21], M[2], M[2],
                                                       Alu.mult))
            p.step("vector", lambda e: e.tensor_tensor(T[:, 21:22], M[3], M[3],
                                                       Alu.mult))
            p.step("vector", lambda e: e.tensor_tensor(T[:, 22:23], C[0], C[1],
                                                       Alu.add))
            p.step("vector", lambda e: e.tensor_tensor(T[:, 23:24], C[2], C[3],
                                                       Alu.add))
            vsync()
            p.step("vector", lambda e: e.tensor_tensor(
                T[:, 12:13], T[:, 16:17], T[:, 17:18], Alu.add))
            p.step("vector", lambda e: e.tensor_tensor(
                T[:, 13:14], T[:, 22:23], T[:, 23:24], Alu.add))
            p.step("vector", lambda e: e.tensor_tensor(
                T[:, 16:17], T[:, 18:19], T[:, 19:20], Alu.add))
            p.step("vector", lambda e: e.tensor_tensor(
                T[:, 17:18], T[:, 20:21], T[:, 21:22], Alu.add))
            vsync()
            p.step("vector", lambda e: e.tensor_scalar(
                T[:, 12:13], T[:, 12:13], 0.25, None, Alu.mult))
            p.step("vector", lambda e: e.tensor_tensor(
                T[:, 14:15], T[:, 16:17], T[:, 17:18], Alu.add))
            vsync()
            p.step("vector", lambda e: e.tensor_scalar(
                T[:, 16:17], T[:, 14:15], 256.0, None, Alu.mult))
            p.step("vector", lambda e: e.tensor_tensor(
                T[:, 17:18], T[:, 12:13], T[:, 12:13], Alu.mult))
            vsync()
            p.step("vector", lambda e: e.tensor_tensor(
                T[:, 16:17], T[:, 16:17], T[:, 13:14], Alu.add))
            vsync()
            p.step("vector", lambda e: e.tensor_scalar(
                T[:, 15:16], T[:, 16:17], 1.0 / 1024.0, None, Alu.mult))
            vsync()
            p.step("vector", lambda e: e.tensor_tensor(
                T[:, 15:16], T[:, 15:16], T[:, 17:18], Alu.subtract))
            vsync()
            p.step("vector", lambda e: e.tensor_scalar(
                T[:, 14:15], T[:, 15:16], EPS, None, Alu.add), inc="sV")
            p.wait("scalar", "sV")
            p.step("scalar", lambda e: e.activation(
                T[:, 15:16], T[:, 14:15], Act.Sqrt), inc="sC")
            p.wait("vector", "sC")
            p.step("vector", lambda e: e.reciprocal(T[:, 8:9], T[:, 15:16]))
            vsync()
            p.step("vector", lambda e: e.tensor_tensor(
                T[:, 9:10], T[:, 8:9], T[:, 8:9], Alu.mult))
            vsync()
            p.step("vector", lambda e: e.tensor_tensor(
                T[:, 9:10], T[:, 9:10], T[:, 14:15], Alu.mult))
            vsync()
            p.step("vector", lambda e: e.tensor_scalar(
                T[:, 9:10], T[:, 9:10], -0.5, 1.5, Alu.mult, Alu.add))
            vsync()
            p.step("vector", lambda e: e.tensor_tensor(
                T[:, 10:11], T[:, 9:10], T[:, 8:9], Alu.mult))
            p.step("vector", lambda e, x=xg: e.tensor_scalar(
                x, x, T[:, 12:13], None, Alu.subtract))
            p.step("vector", lambda e, x=xg: e.tensor_scalar(
                x, x, T[:, 10:11], None, Alu.mult))

        XSA = arf(O_XG, 1024)
        XSB = arf(O_XG + 1024, 1024)
        XNO = {0: (O_XNH, O_XNL), 1: (12288, 14336),
               2: (O_XNH, O_XNL), 3: (29696, 31744)}
        stats_mark = {}
        xn_ready = {}
        prep_state = {"xsb_free": 0, "ap_mark": 0, "p1f": None,
                      "slot_free": {0: ("sP", 0), 1: ("sP", 0)},
                      "ap_fence": {0: 0, 1: 0}}
        T = TN

        def ln_stats_g(tau, g):
            rows0 = tau * TT + g * 128
            p.wait("gpsimd", "sV")
            if g == 0:
                p.wait("gpsimd", "sP", prep_state["xsb_free"])
            p.dma("gpsimd", XSA, x_h[rows0:rows0 + 128, :], "qG")
            fx = qfence("gpsimd")
            p.wait("vector", "fG", fx)
            b = g * 12
            p.step("vector", lambda e, bb=b: e.bn_stats(
                T[:, bb:bb + 6], XSA[:, 0:512]))
            p.step("vector", lambda e, bb=b: e.bn_stats(
                T[:, bb + 6:bb + 12], XSA[:, 512:1024]), inc="sV")
            stats_mark[(tau, g)] = p.n("sV")

        def _tt(d, a, b, op):
            p.step("vector", lambda e, dd=d, aa=a, bb=b, oo=op: e.tensor_tensor(
                T[:, dd:dd + 1], T[:, aa:aa + 1], T[:, bb:bb + 1], oo))

        def _ts(d, a, s1, s2, op1, op2=None):
            if op2 is None:
                p.step("vector", lambda e, dd=d, aa=a: e.tensor_scalar(
                    T[:, dd:dd + 1], T[:, aa:aa + 1], s1, s2, op1))
            else:
                p.step("vector", lambda e, dd=d, aa=a: e.tensor_scalar(
                    T[:, dd:dd + 1], T[:, aa:aa + 1], s1, s2, op1, op2))

        def ln_agg4():
            vsync()
            for g in range(NG):
                s, u = g * 12, 48 + 8 * g
                _tt(u + 0, s + 1, s + 4, Alu.add)
                _tt(u + 1, s + 7, s + 10, Alu.add)
                _tt(u + 2, s + 1, s + 1, Alu.mult)
                _tt(u + 3, s + 4, s + 4, Alu.mult)
                _tt(u + 4, s + 7, s + 7, Alu.mult)
                _tt(u + 5, s + 10, s + 10, Alu.mult)
                _tt(u + 6, s + 2, s + 5, Alu.add)
                _tt(u + 7, s + 8, s + 11, Alu.add)
            vsync()
            for g in range(NG):
                u = 48 + 8 * g
                _tt(80 + g, u + 0, u + 1, Alu.add)
                _tt(u + 6, u + 6, u + 7, Alu.add)
                _tt(u + 0, u + 2, u + 3, Alu.add)
                _tt(u + 1, u + 4, u + 5, Alu.add)
            vsync()
            for g in range(NG):
                u = 48 + 8 * g
                _ts(80 + g, 80 + g, 0.25, None, Alu.mult, None)
                _tt(u + 2, u + 0, u + 1, Alu.add)
            vsync()
            for g in range(NG):
                u = 48 + 8 * g
                _ts(u + 3, u + 2, 256.0, None, Alu.mult, None)
                _tt(u + 4, 80 + g, 80 + g, Alu.mult)
            vsync()
            for g in range(NG):
                u = 48 + 8 * g
                _tt(u + 3, u + 3, u + 6, Alu.add)
            vsync()
            for g in range(NG):
                u = 48 + 8 * g
                _ts(u + 5, u + 3, 1.0 / 1024.0, None, Alu.mult, None)
            vsync()
            for g in range(NG):
                u = 48 + 8 * g
                _tt(u + 5, u + 5, u + 4, Alu.subtract)
            vsync()
            for g in range(NG):
                u = 48 + 8 * g
                _ts(88 + g, u + 5, EPS, None, Alu.add, None)
            p.step("vector", lambda e: e.engine_nop(), inc="sV")
            p.wait("scalar", "sV")
            p.step("scalar", lambda e: e.activation(
                T[:, 92:96], T[:, 88:92], Act.Sqrt), inc="sC")
            p.wait("vector", "sC")
            p.step("vector", lambda e: e.reciprocal(T[:, 84:88], T[:, 92:96]))
            vsync()
            p.step("vector", lambda e: e.tensor_tensor(
                T[:, 48:52], T[:, 84:88], T[:, 84:88], Alu.mult))
            vsync()
            p.step("vector", lambda e: e.tensor_tensor(
                T[:, 48:52], T[:, 48:52], T[:, 88:92], Alu.mult))
            vsync()
            p.step("vector", lambda e: e.tensor_scalar(
                T[:, 48:52], T[:, 48:52], -0.5, 1.5, Alu.mult, Alu.add))
            vsync()
            p.step("vector", lambda e: e.tensor_tensor(
                T[:, 84:88], T[:, 48:52], T[:, 84:88], Alu.mult))
            vsync()

        def ln_apply_load(tau, g):
            slot = XSB if g % 2 == 0 else XSA
            rows0 = tau * TT + g * 128
            s, v = prep_state["slot_free"][g % 2]
            p.wait("gpsimd", s, v)
            p.dma("gpsimd", slot, x_h[rows0:rows0 + 128, :], "qG")
            prep_state["ap_fence"][g % 2] = qfence("gpsimd")

        def ln_apply_v(tau, g):
            slot = XSB if g % 2 == 0 else XSA
            p.wait("vector", "fG", prep_state["ap_fence"][g % 2])
            p.step("vector", lambda e, x=slot, gg=g: e.tensor_scalar(
                x, x, T[:, 80 + gg:81 + gg], None, Alu.subtract))
            p.step("vector", lambda e, x=slot, gg=g: e.tensor_scalar(
                x, x, T[:, 84 + gg:85 + gg], None, Alu.mult), inc="sV")
            prep_state["ap_mark"] = p.n("sV")
            if tau == 0 and g == 0:
                dbg("xn0", slot)

        def ln_apply_pe(tau, g):
            slot = XSB if g % 2 == 0 else XSA
            b0 = (g % 2) * 2
            p.wait("tensor", "sV", prep_state["ap_mark"])
            bank_wait("tensor", b0)
            bank_wait("tensor", b0 + 1)
            for k in range(NK):
                bb = b0 + k // 4
                cc = (k % 4) * 128
                p.step("tensor", lambda e, b=bb, c=cc, kk=k, x=slot: e.transpose(
                    pb[b][:, c:c + 128], x[:, kk * 128:(kk + 1) * 128],
                    IDF[:]), inc="sP" if k == NK - 1 else None)
            prep_state["xsb_free"] = p.n("sP")
            prep_state["slot_free"][g % 2] = ("sP", p.n("sP"))
            trm = p.n("sP")
            oh, ol = XNO[tau]
            p.wait("vector", "sP", trm)
            if prep_state["p1f"] is not None and tau == 1:
                p.wait("vector", "fR", prep_state["p1f"])
            if tau == 3 and g == 0:
                p.wait("vector", "fR", p2_state["fence"])
            for k in range(NK):
                bb = b0 + k // 4
                cc = (k % 4) * 128
                xnh = arb(oh + k * 256, 512)
                xnl = arb(ol + k * 256, 512)
                p.step("vector", lambda e, o=xnh, b=bb, c=cc, gg=g:
                       e.tensor_copy(o[:, gg * 128:(gg + 1) * 128],
                                     pb[b][:, c:c + 128]))
                p.step("vector", lambda e, o=xnl, b=bb, c=cc, gg=g, hh=xnh:
                       e.tensor_tensor(o[:, gg * 128:(gg + 1) * 128],
                                       pb[b][:, c:c + 128],
                                       hh[:, gg * 128:(gg + 1) * 128],
                                       Alu.subtract),
                       inc="sV" if k == NK - 1 else None)
            em = p.n("sV")
            bank_free[b0] = ("sV", em)
            bank_free[b0 + 1] = ("sV", em)
            if g == NG - 1:
                xn_ready[tau] = em

        def prep_step(tau, s):
            if s < NG:
                ln_stats_g(tau, s)
                if s == NG - 1:
                    prep_state["slot_free"][1] = ("sV", stats_mark[(tau, s)])
                    ln_agg4()
                    ln_apply_load(tau, 0)
            elif s == NG:
                ln_apply_v(tau, 0)
                ln_apply_load(tau, 1)
            elif s < NK - 1:
                g = s - NG
                ln_apply_pe(tau, g - 1)
                ln_apply_v(tau, g)
                ln_apply_load(tau, g + 1)
            else:
                ln_apply_pe(tau, 2)
                ln_apply_v(tau, 3)
                ln_apply_pe(tau, 3)

        PB1 = 8192
        p1_slot_fence = {}
        p1_chunks = [(wz_h, wzh_s, wzl_s, k) for k in range(NK)] +                     [(wh_h, whh_s, whl_s, k) for k in range(NK)]
        for j, (w_in, s_hi, s_lo, k) in enumerate(p1_chunks):
            slot = j % 4
            a_in = arf(PB1 + slot * 2048, 1024)
            a_hi = arb(PB1 + slot * 2048 + 1024, 1024)
            a_lo = arb(PB1 + slot * 2048 + 1536, 1024)
            if j >= 4:
                p.wait("sync", "fR", p1_slot_fence[slot])
            p.dma("sync", a_in, w_in[k * 128:(k + 1) * 128, :], "qS")
            fv = qfence("sync")
            p.wait("vector", "fS", fv)
            p.step("vector", lambda e, o=a_hi, i=a_in: e.tensor_copy(o, i))
            p.step("vector", lambda e, o=a_lo, i=a_in, h=a_hi:
                   e.tensor_tensor(o, i, h, Alu.subtract), inc="sV")
            p.wait("scalar", "sV")
            p.dma("scalar", s_hi[k * 128:(k + 1) * 128, :], a_hi, "qR")
            p.dma("scalar", s_lo[k * 128:(k + 1) * 128, :], a_lo, "qR")
            p1_slot_fence[slot] = qfence("scalar")
            if j % 2 == 1 and j // 2 < 8:
                prep_step(0, j // 2)
        p1_out_f = qfence("scalar")
        p.wait("sync", "fR", p1_out_f)
        prep_state["p1f"] = p1_out_f

        def wz_piece(roll, idx):
            o = O_WZ + (roll * 4 + idx) * 512
            return arb(o, 1024).rearrange("p (k j) -> p k j", j=128)

        mixer_scratch = [wzh_s, wzl_s, whh_s, whl_s]
        wz_piece_done = {}
        wz_buf_free = {}
        p2_jobs = [("d", ck) for ck in range(8)] + [("u", u) for u in range(6)]
        p2_keys = [(0, m) for m in range(NK)] + [(1, m) for m in range(NK - 1)]
        p2_state = {"fence": p1_out_f, "adone": {}}

        def p2_stageA(j):
            kind, arg = p2_jobs[j]
            w = 1024 if kind == "u" else 384
            a_in = arf(P2A, w)
            p.wait("gpsimd", "sV")
            if kind == "u":
                p.dma("gpsimd", a_in, ws1_h[arg * 128:(arg + 1) * 128, :], "qG")
            else:
                p.dma("gpsimd", a_in, ws2_h[arg * 128:(arg + 1) * 128, :], "qG")
            fp2 = qfence("gpsimd")
            p.wait("vector", "fG", fp2)
            quant(a_in, P2B, P2C, P2M, w // 16, w, arb(P2E, w), arb(P2F, w))
            p.step("vector", lambda e: e.engine_nop(), inc="sV")
            p2_state["adone"][j] = p.n("sV")

        def p2_stageB(j):
            kind, arg = p2_jobs[j]
            p.wait("tensor", "sV", p2_state["adone"][j])
            if kind == "u":
                bank_wait("tensor", 0)
                bank_wait("tensor", 1)
                for hl, so in ((0, P2E), (1, P2F)):
                    for k in range(NK):
                        p.step("tensor", lambda e, bb=hl, oo=so, kk=k:
                               e.transpose(
                            pb[bb][:].bitcast(BF16)[:, kk * 128:(kk + 1) * 128],
                            arb(oo, 1024)[:, kk * 128:(kk + 1) * 128], IDB[:]),
                            inc="sP" if (hl == 1 and k == NK - 1) else None)
                trc = p.n("sP")
                p.wait("vector", "sP", trc)
                if p2_state["fence"] is not None:
                    p.wait("vector", "fR", p2_state["fence"])
                p.step("vector", lambda e: e.tensor_copy(
                    arb(8192, 1024), pb[0][:].bitcast(BF16)[:, 0:1024]))
                p.step("vector", lambda e: e.tensor_copy(
                    arb(8704, 1024), pb[1][:].bitcast(BF16)[:, 0:1024]),
                    inc="sV")
                evc = p.n("sV")
                bank_free[0] = ("sV", evc)
                bank_free[1] = ("sV", evc)
                p.wait("scalar", "sV", evc)
                agt = ag1a_in if arg < 3 else ag1b_in
                for hl, eo in ((0, 8192), (1, 8704)):
                    col = (2 * (arg % 3) + hl) * 128
                    p.dma("scalar",
                          agt[:, col:col + 128].rearrange(
                              "(k p) j -> p k j", p=128),
                          arb(eo, 1024).rearrange(
                              "p (k j) -> p k j", j=128), "qR")
            else:
                bank_wait("tensor", 0)
                for hl, so in ((0, P2E), (1, P2F)):
                    for lb in range(3):
                        col = (hl * 3 + lb) * 128
                        p.step("tensor", lambda e, cc=col, oo=so, ll=lb:
                               e.transpose(
                            pb[0][:].bitcast(BF16)[:, cc:cc + 128],
                            arb(oo, 384)[:, ll * 128:(ll + 1) * 128], IDB[:]),
                            inc="sP" if (hl == 1 and lb == 2) else None)
                trc = p.n("sP")
                p.wait("vector", "sP", trc)
                if p2_state["fence"] is not None:
                    p.wait("vector", "fR", p2_state["fence"])
                p.step("vector", lambda e: e.tensor_copy(
                    arb(8192, 768), pb[0][:].bitcast(BF16)[:, 0:768]),
                    inc="sV")
                evc = p.n("sV")
                bank_free[0] = ("sV", evc)
                p.wait("scalar", "sV", evc)
                for hl in (0, 1):
                    for lb in range(3):
                        p.dma("scalar",
                              ag2_in[lb * 128:(lb + 1) * 128,
                                     hl * 1024 + arg * 128:
                                     hl * 1024 + (arg + 1) * 128],
                              arb(8192 + (hl * 3 + lb) * 64, 128), "qR")
            p2_state["fence"] = qfence("scalar")

        def p2_emit(key):
            i = p2_keys.index(key)
            if i > 0 and i - 1 < len(p2_jobs):
                p2_stageB(i - 1)
                trig = {7: (ag2_in, ag2_out), 10: (ag1a_in, ag1a_out),
                        13: (ag1b_in, ag1b_out)}.get(i - 1)
                if trig is not None:
                    p.wait("gpsimd", "fR", p2_state["fence"])
                    p.step("gpsimd", lambda e, a=trig[0], b=trig[1]:
                           e.collective_compute("AllGather", Alu.bypass,
                                                replica_groups=RG,
                                                ins=[a], outs=[b]), inc="sG")
            if i < len(p2_jobs):
                p2_stageA(i)
        h_written = {}

        for tau in range(NT):
            if tau == 0:
                dbg("xnh", arb(O_XNH, 4096))
                dbg("xnl", arb(O_XNL, 4096))

            def wz_fetch(tau_, m_):
                rollm_ = (tau_ * NK + m_) % 2
                for idx4, scr in enumerate(mixer_scratch):
                    key = (rollm_, idx4)
                    if key in wz_buf_free:
                        s, v = wz_buf_free[key]
                        p.wait("sync", s, v)
                    dst = wz_piece(rollm_, idx4)
                    srcw = scr.rearrange("(k q) (mm j) -> q k mm j",
                                         q=128, j=128)[:, :, m_, :]
                    p.dma("sync", dst, srcw, "qS")
                wz_piece_done[(tau_, m_)] = qfence("sync")
            if tau == 0:
                wz_fetch(0, 0)
            for m in range(NK):
                rollm = (tau * NK + m) % 2
                if m < NK - 1:
                    wz_fetch(tau, m + 1)
                elif tau < NT - 1:
                    wz_fetch(tau + 1, 0)
                z_done = h_done = None
                for mat in range(2):
                    bank = 4 + (2 * m + mat) % 4
                    bank_wait("tensor", bank)
                    p.wait("tensor", "fS", wz_piece_done[(tau, m)])
                    if m == 0 and mat == 0:
                        p.wait("tensor", "sV", xn_ready[tau])
                    hi_p = wz_piece(rollm, 2 * mat + 0)
                    lo_p = wz_piece(rollm, 2 * mat + 1)
                    xno_h, xno_l = XNO[tau]
                    for k in range(NK):
                        xnh = arb(xno_h + k * 256, 512)
                        xnl = arb(xno_l + k * 256, 512)
                        p.step("tensor", lambda e, b=bank, hp=hi_p, kk=k, r=xnh,
                               s0=(k == 0): e.matmul(
                            pb[b][:, 0:512], hp[:, kk, :], r,
                            start=s0, stop=False))
                        p.step("tensor", lambda e, b=bank, hp=hi_p, kk=k, r=xnl:
                               e.matmul(pb[b][:, 0:512], hp[:, kk, :], r,
                                        start=False, stop=False))
                        p.step("tensor", lambda e, b=bank, lp=lo_p, kk=k, r=xnh,
                               s1=(k == NK - 1): e.matmul(
                            pb[b][:, 0:512], lp[:, kk, :], r,
                            start=False, stop=s1),
                            inc="sP" if k == NK - 1 else None)
                    mm_done = p.n("sP")
                    p.wait("scalar", "sP", mm_done)
                    if mat == 0:
                        zb = arf(O_ZB + (m % 2) * 512, 512)
                        acol0 = AA[:, tau * 4096 + m * 512:
                                   tau * 4096 + (m + 1) * 512]
                        p.step("scalar", lambda e, o=zb, b=bank, mm=m:
                               e.activation(o, pb[b][:, 0:512], Act.Sigmoid,
                                            bias=BZ[:, mm:mm + 1]))
                        p.step("scalar", lambda e, o=acol0, b=bank, mm=m:
                               e.activation(o, pb[b][:, 0:512], Act.Sigmoid,
                                            bias=NBZ[:, mm:mm + 1],
                                            scale=-1.0), inc="sC")
                        z_done = p.n("sC")
                    else:
                        hb = arf(O_HT + (m % 2) * 512, 512)
                        p.step("scalar", lambda e, o=hb, b=bank, mm=m: e.add(
                            o, pb[b][:, 0:512], BH[:, mm:mm + 1]), inc="sC")
                        h_done = p.n("sC")
                    bank_free[bank] = ("sC", p.n("sC"))
                for idx4 in range(4):
                    wz_buf_free[(rollm, idx4)] = ("sP", mm_done)
                p.wait("vector", "sC", h_done)
                zb = arf(O_ZB + (m % 2) * 512, 512)
                hb = arf(O_HT + (m % 2) * 512, 512)
                acol = AA[:, tau * 4096 + m * 512: tau * 4096 + (m + 1) * 512]
                hcol = arf(O_H + tau * 4096 + m * 512, 512)
                bb = arf(O_BB, 512)
                cu = arf(O_CU, 512)
                if tau == 0 and m == 0:
                    dbg("z0", zb, eng="scalar")
                    dbg("ht0", hb, eng="scalar")
                if tau == 2 and m == 0:
                    p.wait("vector", "fR", p2_state["fence"])
                p.step("vector", lambda e, z=zb, h=hb, b=bb:
                       e.tensor_tensor(b, z, h, Alu.mult))
                if m > 0:
                    p.step("vector", lambda e, o=cu, mm=m - 1: e.tensor_copy(
                        AST[:, mm:mm + 1], o[:, 511:512]))
                p.step("vector", lambda e, o=hcol, a=acol, b=bb, mm=m:
                       e.tensor_tensor_scan(o, a, b, HST[:, mm:mm + 1],
                                            Alu.mult, Alu.add))
                p.step("vector", lambda e, o=cu, a=acol, mm=m:
                       e.tensor_tensor_scan(o, a, a, AST[:, mm:mm + 1],
                                            Alu.mult, Alu.bypass))
                p.step("vector", lambda e, o=hcol, mm=m: e.tensor_copy(
                    HST[:, mm:mm + 1], o[:, 511:512]), inc="sV")
                if (tau, m) in p2_keys:
                    p2_emit((tau, m))
                if tau < NT - 1:
                    prep_step(tau + 1, m)
            vsync()
            p.step("vector", lambda e: e.tensor_copy(
                AST[:, 7:8], arf(O_CU + 511, 1)), inc="sV")
            h_written[tau] = p.n("sV")

            if False:
                for kind, arg in p2_sched[tau]:
                    if kind == "u":
                        u = arg
                        a_in = arf(P2A, 1024)
                        p.wait("sync", "sV")
                        p.dma("sync", a_in, ws1_h[u * 128:(u + 1) * 128, :], "qS")
                        fp2 = qfence("sync")
                        p.wait("vector", "fS", fp2)
                        quant(a_in, P2B, P2C, P2M, 64, 1024,
                              arb(P2E, 1024), arb(P2F, 1024))
                        p.step("vector", lambda e: e.engine_nop(), inc="sV")
                        p.wait("scalar", "sV")
                        p.dma("scalar", ag1_in[u * 128:(u + 1) * 128, 0:1024],
                              arb(P2E, 1024), "qR")
                        p.dma("scalar", ag1_in[u * 128:(u + 1) * 128, 1024:2048],
                              arb(P2F, 1024), "qR")
                    else:
                        c0, w = arg
                        a_in = arf(P2A, w)
                        p.wait("sync", "sV")
                        p.dma("sync", a_in, ws2_h[:, c0:c0 + w], "qS")
                        fp2 = qfence("sync")
                        p.wait("vector", "fS", fp2)
                        quant(a_in, P2B, P2C, P2M, w // 16, w,
                              arb(P2E, w), arb(P2F, w))
                        p.step("vector", lambda e: e.engine_nop(), inc="sV")
                        p.wait("scalar", "sV")
                        p.dma("scalar", ag2_in[:, c0:c0 + w], arb(P2E, w), "qR")
                        p.dma("scalar", ag2_in[:, DI + c0:DI + c0 + w],
                              arb(P2F, w), "qR")
        dbg("a_all", AA)
        dbg("h_allM", arf(O_H, 16384))
        dbg("ast", AST[:])
        dbg("hst", HST[:])
        agw_done = 3

        p.wait("sync", "sP", prep_state["xsb_free"])
        p.dma("sync", arf(O_D, 1024), ldt_h, "qS")
        fld = qfence("sync")
        p.wait("scalar", "fS", fld)
        p.step("scalar", lambda e: e.activation(
            arf(O_D, 1024), arf(O_D, 1024), Act.Exp), inc="sC")
        d_ready = p.n("sC")

        p.wait("tensor", "sV", h_written[NT - 1])
        bank_wait("tensor", 4)
        p.step("tensor", lambda e: e.transpose(
            pb[4][0:8, 0:128], AST[:], IDF[:]))
        p.step("tensor", lambda e: e.transpose(
            pb[4][0:8, 128:256], HST[:], IDF[:]), inc="sP")
        p.wait("vector", "sP")
        p.step("vector", lambda e: e.tensor_copy(
            AR_t[0:8, O_GAT:O_GAT + 256], pb[4][0:8, 0:256]), inc="sV")
        bank_free[4] = ("sV", p.n("sV"))
        p.wait("gpsimd", "sV")
        p.dma("gpsimd", ccm_in[0:1, :].rearrange("a (c q) -> (a c) q", q=128),
              AR_t[0:8, O_GAT:O_GAT + 128], "qG")
        p.dma("gpsimd", ccm_in[1:2, :].rearrange("a (c q) -> (a c) q", q=128),
              AR_t[0:8, O_GAT + 128:O_GAT + 256], "qG")
        p.wait("gpsimd", "qG")
        p.step("gpsimd", lambda e: e.collective_compute(
            "AllReduce" if False else "AllGather", Alu.bypass,
            replica_groups=RG, ins=[ccm_in], outs=[ccm_out]), inc="sG")
        p.wait("gpsimd", "sG")
        p.dma("gpsimd", AR_t[0:16, O_GAT:O_GAT + 1024], ccm_out, "qG")
        fgat = qfence("gpsimd")
        p.wait("tensor", "fG", fgat)
        bank_wait("tensor", 0)
        for c in range(8):
            p.step("tensor", lambda e, cc=c: e.transpose(
                pb[0][:, cc * 16:(cc + 1) * 16],
                AR_t[0:16, O_GAT + cc * 128:O_GAT + (cc + 1) * 128],
                IDF[0:16, 0:16]), inc="sP" if c == 7 else None)
        p.wait("vector", "sP")
        SEQ = arf(O_SEQ, 128)
        p.step("vector", lambda e: e.tensor_copy(SEQ, pb[0][:, 0:128]),
               inc="sV")
        bank_free[0] = ("sV", p.n("sV"))
        vsync()
        PRF = arf(O_PRF, 64)
        for c in range(8):
            s3 = SEQ[:, c * 16:(c + 1) * 16].rearrange(
                "p (r two) -> p r two", two=2)
            p.step("vector", lambda e, cc=c, a=s3: e.tensor_tensor_scan(
                PRF[:, cc * 8:(cc + 1) * 8], a[:, :, 0:1].squeeze(2),
                a[:, :, 1:2].squeeze(2), 0.0, Alu.mult, Alu.add))
        vsync()
        p.step("vector", lambda e: e.tensor_tensor(
            arf(O_GAT, 64), PRF, SEL[:], Alu.mult))
        vsync()
        p.step("vector", lambda e: e.tensor_reduce(
            FST[:].rearrange("p (c o) -> p c o", o=1),
            arf(O_GAT, 64).rearrange("p (c r) -> p c r", r=8), AX, Alu.add),
            inc="sV")
        vsync()
        dbg("seq", SEQ)
        dbg("prf", PRF)
        dbg("fst", FST[:])

        p.wait("vector", "sC", d_ready)

        xg_used = {}
        for tau in range(NT):
            for m in range(NK):
                fb = arf(O_F + (m % 2) * 512, 512)
                acol = AA[:, tau * 4096 + m * 512: tau * 4096 + (m + 1) * 512]
                hcol = arf(O_H + tau * 4096 + m * 512, 512)
                p.step("vector", lambda e, o=fb, a=acol, mm=m:
                       e.tensor_tensor_scan(o, a, a, FST[:, mm:mm + 1],
                                            Alu.mult, Alu.bypass))
                p.step("vector", lambda e, h=hcol, f=fb: e.tensor_tensor(
                    h, h, f, Alu.add))
                p.step("vector", lambda e, o=fb, mm=m: e.tensor_copy(
                    FST[:, mm:mm + 1], o[:, 511:512]), inc="sV")
                if tau == 0 and m == 0:
                    dbg("fb0", fb)
            hf_done = p.n("sV")

            p.wait("tensor", "sV", hf_done)
            for g in range(NG):
                idx = tau * NG + g
                slot = g % 2
                xx = arf(O_XXG + slot * 1024, 1024)
                if idx >= 2:
                    p.wait("gpsimd", "sV", xg_used[idx - 2])
                rows0 = tau * TT + g * 128
                p.dma("gpsimd", xx, x_h[rows0:rows0 + 128, :], "qG")
                xload = qfence("gpsimd")
                b0, b1 = 2 * (g % 2), 2 * (g % 2) + 1
                bank_wait("tensor", b0)
                bank_wait("tensor", b1)
                for k in range(NK):
                    b = b0 if k < 4 else b1
                    p.step("tensor", lambda e, bb=b, kk=k, gg=g, t=tau:
                           e.transpose(
                        pb[bb][:, (kk % 4) * 128:(kk % 4 + 1) * 128],
                        arf(O_H + t * 4096 + kk * 512 + gg * 128, 128),
                        IDF[:]), inc="sP" if k == NK - 1 else None)
                trg = p.n("sP")
                slot = g % 2
                htm = arf(O_HTM + slot * 1024, 1024)
                xx = arf(O_XXG + slot * 1024, 1024)
                ycol = AA[:, tau * 4096 + g * 1024: tau * 4096 + (g + 1) * 1024]
                p.wait("vector", "sP", trg)
                p.step("vector", lambda e, o=htm, bb=b1: e.tensor_copy(
                    o[:, 512:1024], pb[bb][:, 0:512]))
                p.step("vector", lambda e, o=htm, bb=b0: e.tensor_copy(
                    o[:, 0:512], pb[bb][:, 0:512]), inc="sV")
                if tau == 0 and g == 0:
                    dbg("htm0", htm)
                bank_free[b0] = ("sV", p.n("sV"))
                bank_free[b1] = ("sV", p.n("sV"))
                p.wait("vector", "fG", xload)
                p.step("vector", lambda e, y=ycol, h=htm: e.tensor_tensor(
                    y[:, 0:1], h[:, 1023:1024], h[:, 1:2], Alu.add))
                p.step("vector", lambda e, y=ycol, h=htm: e.tensor_tensor(
                    y[:, 1023:1024], h[:, 1022:1023], h[:, 0:1], Alu.add))
                p.step("vector", lambda e, y=ycol, h=htm: e.tensor_tensor(
                    y[:, 1:1023], h[:, 0:1022], h[:, 2:1024], Alu.add))
                p.step("vector", lambda e, y=ycol, h=htm:
                       e.scalar_tensor_tensor(y, h, -2.0, y, Alu.mult, Alu.add))
                p.step("vector", lambda e, y=ycol: e.tensor_tensor(
                    y, y, arf(O_D, 1024), Alu.mult))
                p.step("vector", lambda e, y=ycol, h=htm: e.tensor_tensor(
                    y, y, h, Alu.add))
                p.step("vector", lambda e, y=ycol, x=xx: e.tensor_tensor(
                    y, y, x, Alu.add), inc="sV")
                xg_used[idx] = p.n("sV")

        dbg("h_allX", arf(O_H, 16384))
        dbg("yprov", AA)
        if DEBUG:
            fdbg = qfence("scalar")
            p.wait("vector", "fR", fdbg)
            p.wait("sync", "fR", fdbg)

        def wg_piece_ap(slot, W):
            return arb(O_WG + slot * 2048, 8 * W).rearrange(
                "p (k j) -> p k j", j=W)

        hq_state = {"trc": 0, "evc": 0}

        def emit_hq_tr(pending):
            n0, W, qcnt = pending
            nic = W // 128
            npair = 2 * nic
            p.wait("tensor", "sV", qcnt)
            for pr in range(npair):
                hl, ic = pr // nic, pr % nic
                bank = 4 + pr // 2
                if pr % 2 == 0:
                    bank_wait("tensor", bank)
                for g in range(NG):
                    srct = arb(O_HQS + hl * 1024 + g * (W // 2), W)[
                        :, ic * 128:(ic + 1) * 128]
                    p.step("tensor", lambda e, b=bank, s=srct, gg=g,
                           c=(pr % 2) * 512: e.transpose(
                        pb[b][:].bitcast(BF16)[:, c + gg * 128:c + (gg + 1) * 128],
                        s, IDB[:]),
                        inc="sP" if (pr == npair - 1 and g == NG - 1) else None)
            trc = p.n("sP")
            hq_state["trc"] = trc
            p.wait("scalar", "sP", trc)
            for pr in range(npair):
                hl, ic = pr // nic, pr % nic
                bank = 4 + pr // 2
                ig = n0 // 128 + ic
                dsto = (O_HQTH if hl == 0 else O_HQTL) + ig * 256
                p.step("scalar", lambda e, o=arb(dsto, 512), b=bank,
                       c=(pr % 2) * 512: e.activation(
                    o, pb[b][:].bitcast(BF16)[:, c:c + 512], Act.Copy),
                    inc="sC")
            for j in range(4):
                bank_free[4 + j] = ("sC", p.n("sC"))
            hq_state["evc"] = p.n("sC")

        def f_apply_v(t1, g):
            rows0 = t1 * TT + g * 128
            p.wait("scalar", "sV")
            p.dma("scalar", XSB, x_h[rows0:rows0 + 128, :], "qR")
            fx = qfence("scalar")
            p.wait("vector", "fR", fx)
            p.step("vector", lambda e, gg=g: e.tensor_scalar(
                XSB, XSB, T[:, 80 + gg:81 + gg], None, Alu.subtract))
            p.step("vector", lambda e, gg=g: e.tensor_scalar(
                XSB, XSB, T[:, 84 + gg:85 + gg], None, Alu.mult), inc="sV")

        def f_prep_step(t1, s):
            if s < NG:
                ln_stats_g(t1, s)
                if s == NG - 1:
                    ln_agg4()
            else:
                g = s - NG
                f_apply_v(t1, g)
                quant(XSB, 4096, g * 1024, 5120, 64, 1024)
                p.step("vector", lambda e: e.engine_nop(), inc="sV")

        f_stash_ready = {}
        FPREP_SCHED = {1: 0, 3: 1, 5: 2, 7: 3, 9: 4, 12: 5, 15: 6, 18: 7}

        p.wait("sync", "sG", agw_done)
        if DEBUG:
            p.wait("scalar", "sG", agw_done)
            p.dma("scalar", dbg_tensors["agin1"][:, 0:768], ag1a_in, "qR")
            p.dma("scalar", dbg_tensors["agin1"][:, 768:1536], ag1b_in, "qR")
            p.dma("scalar", dbg_tensors["agout1"][:, 0:768], ag1a_out, "qR")
            p.dma("scalar", dbg_tensors["agout1"][:, 768:1536], ag1b_out, "qR")
            p.dma("scalar", dbg_tensors["agin2"], ag2_in, "qR")
            p.dma("scalar", dbg_tensors["agout2"], ag2_out, "qR")
        p.wait("sync", "sV")
        p.wait("scalar", "sV")

        fxg_free = {}
        wg_slot_free = {}
        store_done = {}
        for tau in range(NT):
            xq_ev = {}
            for g in range(NG):
                if tau == 0:
                    idx = tau * NG + g
                    slot = g % 2
                    xg = arf(O_FXG + slot * 1024, 1024)
                    if idx >= 2:
                        p.wait("scalar", "sP", fxg_free[idx - 2])
                    rows0 = tau * TT + g * 128
                    p.dma("scalar", xg, x_h[rows0:rows0 + 128, :], "qR")
                    fx = qfence("scalar")
                    p.wait("vector", "fR", fx)
                    layer_norm(xg)
                    quant(xg, O_S1, O_S2, O_S3, 64, 1024)
                    p.step("vector", lambda e, x=xg: e.tensor_copy(
                        x, arf(O_S2, 1024)), inc="sV")
                    p.wait("tensor", "sV", p.n("sV"))
                else:
                    xg = arf(g * 1024, 1024)
                    if g == 0:
                        p.wait("tensor", "sV", f_stash_ready[tau])
                if g > 0:
                    p.wait("tensor", "sV", xq_ev[g - 1])
                for k in range(NK):
                    if g == 0:
                        bank_wait("tensor", k)
                    p.step("tensor", lambda e, b=k, gg=g, x=xg: e.transpose(
                        pb[b][:, gg * 128:(gg + 1) * 128],
                        x[:, b * 128:(b + 1) * 128], IDF[:]),
                        inc="sP" if k == NK - 1 else None)
                fxg_free[tau * NG + g] = p.n("sP")
                p.wait("scalar", "sP", fxg_free[tau * NG + g])
                for k in range(NK):
                    xqh = arb(O_XQH + k * 256 + g * 64, 128)
                    p.step("scalar", lambda e, o=xqh, b=k, gg=g: e.activation(
                        o, pb[b][:, gg * 128:(gg + 1) * 128], Act.Copy),
                        inc="sC" if k == NK - 1 else None)
                xq_him = p.n("sC")
                p.wait("vector", "sC", xq_him)
                for k in range(NK):
                    xqh = arb(O_XQH + k * 256 + g * 64, 128)
                    xql = arb(O_XQL + k * 256 + g * 64, 128)
                    p.step("vector", lambda e, o=xql, b=k, gg=g, h=xqh:
                           e.tensor_tensor(o, pb[b][:, gg * 128:(gg + 1) * 128],
                                           h, Alu.subtract),
                           inc="sV" if k == NK - 1 else None)
                xq_ev[g] = p.n("sV")
                if g == NG - 1:
                    for k in range(NK):
                        bank_free[k] = ("sV", xq_ev[g])

            pending = None
            for ni, (n0, W) in enumerate(NSL):
                for mat in range(2):
                    sA, sB = 2 * mat, 2 * mat + 1
                    for sslot, hl in ((sA, 0), (sB, 1)):
                        if sslot in wg_slot_free:
                            p.wait("sync", "sP", wg_slot_free[sslot])
                        dst = wg_piece_ap(sslot, W)
                        for uu in range(W // 128):
                            ju = 22 * mat + n0 // 128 + uu
                            cj, lj = ju // 6, ju % 6
                            agt = ag1a_out if lj < 3 else ag1b_out
                            col = (2 * (lj % 3) + hl) * 128
                            src = agt[cj * D:(cj + 1) * D,
                                      col:col + 128].rearrange(
                                "(k q) j -> q k j", q=128)
                            p.dma("sync", dst[:, :, uu * 128:(uu + 1) * 128],
                                  src, "qS")
                    piece_f = qfence("sync")
                    if DEBUG and tau == 0 and ni == 0 and mat == 0:
                        p.wait("scalar", "fS", piece_f)
                        p.dma("scalar", dbg_tensors["piece0"],
                              arb(O_WG, 4096), "qR")
                    p.wait("tensor", "fS", piece_f)
                    hi_pc = wg_piece_ap(sA, W)
                    lo_pc = wg_piece_ap(sB, W)
                    g_stop = {}
                    for g in range(NG):
                        bank = 4 * mat + g
                        bank_wait("tensor", bank)
                        if ni == 0 and mat == 0:
                            p.wait("tensor", "sV", xq_ev[g])
                        for ph, (so, pc) in enumerate(
                                [(O_XQH, hi_pc), (O_XQL, hi_pc),
                                 (O_XQH, lo_pc)]):
                            for k in range(NK):
                                stat = arb(so + k * 256 + g * 64, 128)
                                p.step("tensor", lambda e, b=bank, st=stat,
                                       mv=pc, kk=k, ww=W,
                                       s0=(ph == 0 and k == 0),
                                       s1=(ph == 2 and k == NK - 1): e.matmul(
                                    pb[b][:, 0:ww], st, mv[:, kk, :],
                                    start=s0, stop=s1),
                                    inc="sP" if (ph == 2 and k == NK - 1)
                                    else None)
                        g_stop[g] = p.n("sP")
                    wg_slot_free[sA] = p.n("sP")
                    wg_slot_free[sB] = p.n("sP")
                    if mat == 0:
                        for g in range(NG):
                            gs = arf(O_GS + g * 512, W)
                            p.wait("scalar", "sP", g_stop[g])
                            p.step("scalar", lambda e, o=gs, b=g, ww=W:
                                   e.activation(o, pb[b][:, 0:ww], Act.Silu),
                                   inc="sC")
                            bank_free[g] = ("sC", p.n("sC"))
                    else:
                        for g in range(NG):
                            gs = arf(O_GS + g * 512, W)
                            h2 = arf(O_H2 + g * 512, W)
                            p.wait("vector", "sP", g_stop[g])
                            p.wait("vector", "sC")
                            p.step("vector", lambda e, o=h2, b=4 + g, s=gs,
                                   ww=W: e.tensor_tensor(
                                o, pb[b][:, 0:ww], s, Alu.mult),
                                inc="sV")
                            bank_free[4 + g] = ("sV", p.n("sV"))
                if pending is not None:
                    emit_hq_tr(pending)
                    pending = None
                p.wait("vector", "sP", hq_state["trc"])
                if W == 512:
                    quant(arf(O_H2, 2048), O_FXG, O_S1, O_S3, 128, 2048,
                          arb(O_HQS, 2048), arb(O_HQS + 1024, 2048))
                else:
                    for g in range(NG):
                        quant(arf(O_H2 + g * 512, W), O_FXG, O_S1, O_S3,
                              W // 16, W,
                              arb(O_HQS + g * (W // 2), W),
                              arb(O_HQS + 1024 + g * (W // 2), W))
                p.step("vector", lambda e: e.engine_nop(), inc="sV")
                pending = (n0, W, p.n("sV"))
            emit_hq_tr(pending)
            pending = None
            hqT_done = p.n("sC")

            wd_free = {}
            for i in range(NI):
                roll = i % 2
                if i >= 2:
                    p.wait("sync", "sP", wd_free[i - 2])
                if i == 0:
                    p.wait("sync", "sC", hqT_done)
                dsth = arb(O_WD + roll * 1024, 1024)
                dstl = arb(O_WD + roll * 1024 + 512, 1024)
                cc, lb = (i // 3, i % 3) if i < 18 else \
                    (6 + (i - 18) // 2, (i - 18) % 2)
                r0 = cc * 384 + lb * 128
                p.dma("sync", dsth, ag2_out[r0:r0 + 128, 0:1024], "qS")
                p.dma("sync", dstl, ag2_out[r0:r0 + 128, 1024:2048], "qS")
                dwd = qfence("sync")
                if DEBUG and tau == 0 and i == 0:
                    p.wait("scalar", "fS", dwd)
                    p.dma("scalar", dbg_tensors["wd0"], arb(O_WD, 2048), "qR")
                p.wait("tensor", "fS", dwd)
                if i == 0:
                    for b in range(8):
                        bank_wait("tensor", b)
                mh = dsth.rearrange("p (a j) -> p a j", a=2)
                ml = dstl.rearrange("p (a j) -> p a j", a=2)
                for g in range(NG):
                    sth = arb(O_HQTH + i * 256, 512)[:, g * 128:(g + 1) * 128]
                    stl = arb(O_HQTL + i * 256, 512)[:, g * 128:(g + 1) * 128]
                    for half in range(2):
                        b = 2 * g + half
                        last = (i == NI - 1)
                        p.step("tensor", lambda e, bb=b, s=sth, m=mh, hh=half,
                               s0=(i == 0): e.matmul(
                            pb[bb][:, 0:512], s, m[:, hh, :],
                            start=s0, stop=False))
                        p.step("tensor", lambda e, bb=b, s=sth, m=ml, hh=half:
                               e.matmul(pb[bb][:, 0:512], s, m[:, hh, :],
                                        start=False, stop=False))
                        p.step("tensor", lambda e, bb=b, s=stl, m=mh, hh=half,
                               s1=last: e.matmul(
                            pb[bb][:, 0:512], s, m[:, hh, :],
                            start=False, stop=s1),
                            inc="sP" if (g == NG - 1 and half == 1) else None)
                wd_free[i] = p.n("sP")
                if tau < NT - 1 and i in FPREP_SCHED:
                    f_prep_step(tau + 1, FPREP_SCHED[i])
                    if FPREP_SCHED[i] == 7:
                        f_stash_ready[tau + 1] = p.n("sV")
            down_done = p.n("sP")

            p.wait("vector", "sP", down_done)
            for g in range(NG):
                idx = tau * NG + g
                ybuf = arf(O_WG + (g % 2) * 1024, 1024)
                if idx >= 2:
                    p.wait("vector", "fR", store_done[idx - 2])
                ycol = AA[:, tau * 4096 + g * 1024: tau * 4096 + (g + 1) * 1024]
                p.step("vector", lambda e, o=ybuf, b=2 * g, y=ycol:
                       e.tensor_tensor(o[:, 0:512], pb[b][:, 0:512],
                                       y[:, 0:512], Alu.add))
                p.step("vector", lambda e, o=ybuf, b=2 * g + 1, y=ycol:
                       e.tensor_tensor(o[:, 512:1024], pb[b][:, 0:512],
                                       y[:, 512:1024], Alu.add), inc="sV")
                bank_free[2 * g] = ("sV", p.n("sV"))
                bank_free[2 * g + 1] = ("sV", p.n("sV"))
                p.wait("scalar", "sV")
                rows0 = tau * TT + g * 128
                p.dma("scalar", y_h[rows0:rows0 + 128, :], ybuf, "qR")
                store_done[idx] = qfence("scalar")
            for sslot in range(4):
                wg_slot_free.pop(sslot, None)
            p.wait("sync", "fR", p.n("fR"))

        p.wait("scalar", "qR", p.n("qR"))

        block = es.enter_context(nc.Block())
        p.emit(block)

    return nc


_NC_CACHE = None


def kernel(L0_attn_x, L0_attn_Wz, L0_attn_bz, L0_attn_Wh, L0_attn_bh,
           L0_attn_log_D, L0_ffn_w_gate, L0_ffn_w_up, L0_ffn_w_down):
    global _NC_CACHE
    x = np.ascontiguousarray(np.asarray(L0_attn_x, dtype=np.float32)
                             ).reshape(16384, 1024)
    wz = np.ascontiguousarray(np.asarray(L0_attn_Wz, dtype=np.float32))
    wh = np.ascontiguousarray(np.asarray(L0_attn_Wh, dtype=np.float32))
    bz = np.asarray(L0_attn_bz, dtype=np.float32)
    bh = np.asarray(L0_attn_bh, dtype=np.float32)
    ld = np.asarray(L0_attn_log_D, dtype=np.float32)
    wg = np.ascontiguousarray(np.asarray(L0_ffn_w_gate, dtype=np.float32))
    wu = np.ascontiguousarray(np.asarray(L0_ffn_w_up, dtype=np.float32))
    wd = np.ascontiguousarray(np.asarray(L0_ffn_w_down, dtype=np.float32))

    bzp = np.ascontiguousarray(bz.reshape(8, 128).T)
    bhp = np.ascontiguousarray(bh.reshape(8, 128).T)
    ldt = np.ascontiguousarray(np.tile(ld[None, :], (128, 1)))
    idf = np.eye(128, dtype=np.float32)
    idb = np.eye(128, dtype=ml_dtypes.bfloat16)

    units = [wg[i * 128:(i + 1) * 128, :] for i in range(22)] + \
            [wu[i * 128:(i + 1) * 128, :] for i in range(22)] + \
            [np.zeros((128, 1024), np.float32)] * 4

    if _NC_CACHE is None:
        _NC_CACHE = build_nc()
    nc = _NC_CACHE

    in_maps = []
    for c in range(NCORES):
        onehot = np.zeros(8, np.float32)
        if c > 0:
            onehot[c - 1] = 1.0
        sel = np.ascontiguousarray(np.tile(onehot[None, :], (128, 8)))
        wsh1 = np.ascontiguousarray(
            np.concatenate(units[6 * c:6 * c + 6], axis=0))
        if c < 6:
            wsl = wd[:, c * 384:(c + 1) * 384]
        else:
            wsl = np.concatenate(
                [wd[:, 2304 + (c - 6) * 256: 2304 + (c - 5) * 256],
                 np.zeros((1024, 128), np.float32)], axis=1)
        wsh2 = np.ascontiguousarray(wsl)
        in_maps.append({
            "x": np.ascontiguousarray(x[c * TC:(c + 1) * TC, :]),
            "wz": wz, "wh": wh, "bzp": bzp, "bhp": bhp, "ldt": ldt,
            "wsh1": wsh1, "wsh2": wsh2, "idf": idf, "idb": idb, "sel": sel,
        })

    res = run_bass_kernel_spmd(nc, in_maps, core_ids=list(range(NCORES)))
    out = np.concatenate([res.results[c]["y"] for c in range(NCORES)], axis=0)
    return out.reshape(1, 16384, 1024)


# version 33
# speedup vs baseline: 1.0206x; 1.0204x over previous
import numpy as np
import ml_dtypes
import concourse.bass as bass
import concourse.mybir as mybir
from concourse.bass_utils import run_bass_kernel_spmd
from concourse.replica_groups import maybe_share_collective_output_space

F32 = mybir.dt.float32
BF16 = mybir.dt.bfloat16
I32 = mybir.dt.int32
I16 = mybir.dt.int16
Alu = mybir.AluOpType
Act = mybir.ActivationFunctionType
AX = mybir.AxisListType.X

NCORES = 8
TC = 2048
D = 1024
DI = 2816
NT = 4
TT = 512
NG = 4
NK = 8
NI = 22
EPS = 1e-5
MASK_HI = -4194304
HALF_DOWN = 0x1FFFFF
ABS_MASK = 0x7FFFFFFF
SIGN_MASK = -2147483648

NSL = [(0, 512), (512, 512), (1024, 512), (1536, 512), (2048, 512), (2560, 256)]
RG = [[0, 1, 2, 3, 4, 5, 6, 7]]

ARN = 34816
O_H = 0
O_XG = 16384
O_XNH = 18432
O_XNL = 20480
O_WZ = 22528
O_ZB = 26624
O_HT = 27648
O_BB = 28672
O_CU = 29184
O_P2 = 29696
P2A = O_P2
P2B = O_P2 + 1024
P2C = O_P2 + 2048
P2M = O_P2 + 3072
P2E = O_P2 + 3584
P2F = O_P2 + 4096
O_D = 16384
O_HTM = 17408
O_F = 19456
O_GAT = 20480
O_SEQ = 21504
O_PRF = 21632
O_XXG = 21760
O_FXG = 0
O_S1 = 2048
O_S2 = 3072
O_S3 = 4096
O_XQH = 5120
O_XQL = 7168
O_WG = 9216
O_HQS = 17408
O_HQTH = 19456
O_HQTL = 25088
O_WD = 30720
O_H2 = 30720
O_GS = 32768

ENGINES = ["sync", "scalar", "vector", "tensor", "gpsimd"]
DEBUG = False


class Plan:
    def __init__(self, nc):
        self.nc = nc
        self.steps = {e: [] for e in ENGINES}
        self.sems = {}
        self.cnt = {}

    def sem(self, name, es):
        h = es.enter_context(self.nc.semaphore(name))
        self.sems[name] = h
        self.cnt[name] = 0

    def step(self, eng, f, inc=None, n=1):
        if inc is not None:
            self.cnt[inc] += n
        self.steps[eng].append(("op", f, inc, n))

    def dma(self, eng, out, in_, inc, **kw):
        self.cnt[inc] += 16
        self.steps[eng].append(
            ("op", (lambda e, o=out, i=in_, k=dict(kw):
                    e.dma_start(out=o, in_=i, **k)), inc, 16))

    def wait(self, eng, sem, val=None):
        v = self.cnt[sem] if val is None else val
        self.steps[eng].append(("wait", sem, v, 0))

    def n(self, sem):
        return self.cnt[sem]

    def emit(self, block):
        for eng in ENGINES:
            steps = self.steps[eng]
            if not steps:
                continue
            deco = getattr(block, eng)

            def make(steps=steps):
                def body(e):
                    for kind, a, b, c in steps:
                        if kind == "wait":
                            e.wait_ge(self.sems[a], b)
                        else:
                            inst = a(e)
                            if b is not None and inst is not None:
                                inst.then_inc(self.sems[b], c)
                return body
            deco(make())


def build_nc():
    nc = bass.Bass("TRN2", target_bir_lowering=False, debug=False,
                   detect_race_conditions=False)

    x_h = nc.dram_tensor("x", [TC, D], F32, kind="ExternalInput").ap()
    wz_h = nc.dram_tensor("wz", [D, D], F32, kind="ExternalInput").ap()
    wh_h = nc.dram_tensor("wh", [D, D], F32, kind="ExternalInput").ap()
    bz_h = nc.dram_tensor("bzp", [128, 8], F32, kind="ExternalInput").ap()
    bh_h = nc.dram_tensor("bhp", [128, 8], F32, kind="ExternalInput").ap()
    ldt_h = nc.dram_tensor("ldt", [128, D], F32, kind="ExternalInput").ap()
    ws1_h = nc.dram_tensor("wsh1", [768, D], F32, kind="ExternalInput").ap()
    ws2_h = nc.dram_tensor("wsh2", [D, 384], F32, kind="ExternalInput").ap()
    idf_h = nc.dram_tensor("idf", [128, 128], F32, kind="ExternalInput").ap()
    idb_h = nc.dram_tensor("idb", [128, 128], BF16, kind="ExternalInput").ap()
    sel_h = nc.dram_tensor("sel", [128, 64], F32, kind="ExternalInput").ap()
    y_h = nc.dram_tensor("y", [TC, D], F32, kind="ExternalOutput").ap()

    wzh_s = nc.dram_tensor("wzh_s", [D, D], BF16, kind="Internal").ap()
    wzl_s = nc.dram_tensor("wzl_s", [D, D], BF16, kind="Internal").ap()
    whh_s = nc.dram_tensor("whh_s", [D, D], BF16, kind="Internal").ap()
    whl_s = nc.dram_tensor("whl_s", [D, D], BF16, kind="Internal").ap()
    sp1 = maybe_share_collective_output_space("AllGather", RG)
    spl = maybe_share_collective_output_space("AllGather", RG,
                                              prefer_local=True)
    ag1a_in = nc.dram_tensor("ag1a_in", [D, 768], BF16, kind="Internal").ap()
    ag1a_out = nc.dram_tensor("ag1a_out", [8 * D, 768], BF16, kind="Internal",
                              addr_space=sp1).ap()
    ag1b_in = nc.dram_tensor("ag1b_in", [D, 768], BF16, kind="Internal").ap()
    ag1b_out = nc.dram_tensor("ag1b_out", [8 * D, 768], BF16, kind="Internal",
                              addr_space=sp1).ap()
    ag2_in = nc.dram_tensor("ag2_in", [384, 2048], BF16, kind="Internal").ap()
    ag2_out = nc.dram_tensor("ag2_out", [3072, 2048], BF16, kind="Internal",
                             addr_space=sp1).ap()
    ccm_in = nc.dram_tensor("ccm_in", [2, D], F32, kind="Internal").ap()
    ccm_out = nc.dram_tensor("ccm_out", [16, D], F32, kind="Internal",
                             addr_space=spl).ap()

    dbg_tensors = {}

    def mkdbg(name, shape, dt=F32):
        dbg_tensors[name] = nc.dram_tensor("dbg_" + name, shape, dt,
                                           kind="ExternalOutput").ap()

    if DEBUG:
        mkdbg("xn0", [128, 1024])
        mkdbg("xnh", [128, 4096], BF16)
        mkdbg("xnl", [128, 4096], BF16)
        mkdbg("z0", [128, 512])
        mkdbg("ht0", [128, 512])
        mkdbg("a_all", [128, 16384])
        mkdbg("h_allM", [128, 16384])
        mkdbg("ast", [128, 8])
        mkdbg("hst", [128, 8])
        mkdbg("fst", [128, 8])
        mkdbg("seq", [128, 128])
        mkdbg("prf", [128, 64])
        mkdbg("h_allX", [128, 16384])
        mkdbg("htm0", [128, 1024])
        mkdbg("fb0", [128, 512])
        mkdbg("yprov", [128, 16384])
        mkdbg("xq0", [128, 1024])
        mkdbg("xqh", [128, 4096], BF16)
        mkdbg("gs0", [128, 2048])
        mkdbg("h20", [128, 2048])
        mkdbg("hqs", [128, 2048], BF16)
        mkdbg("hqth", [128, 11264], BF16)
        mkdbg("agin1", [1024, 1536], BF16)
        mkdbg("agout1", [8192, 1536], BF16)
        mkdbg("agin2", [384, 2048], BF16)
        mkdbg("agout2", [3072, 2048], BF16)
        mkdbg("piece0", [128, 4096], BF16)
        mkdbg("wd0", [128, 2048], BF16)

    from contextlib import ExitStack
    es = ExitStack()
    with es:
        AR_t = es.enter_context(nc.sbuf_tensor("AR", [128, ARN], F32))
        AA_t = es.enter_context(nc.sbuf_tensor("AA", [128, 16384], F32))
        IDF = es.enter_context(nc.sbuf_tensor("IDF", [128, 128], F32))
        IDB = es.enter_context(nc.sbuf_tensor("IDB", [128, 128], BF16))
        BZ = es.enter_context(nc.sbuf_tensor("BZ", [128, 8], F32))
        NBZ = es.enter_context(nc.sbuf_tensor("NBZ", [128, 8], F32))
        BH = es.enter_context(nc.sbuf_tensor("BH", [128, 8], F32))
        SEL = es.enter_context(nc.sbuf_tensor("SEL", [128, 64], F32))
        HST = es.enter_context(nc.sbuf_tensor("HST", [128, 8], F32))
        AST = es.enter_context(nc.sbuf_tensor("AST", [128, 8], F32))
        FST = es.enter_context(nc.sbuf_tensor("FST", [128, 8], F32))
        SC = es.enter_context(nc.sbuf_tensor("SC", [128, 128], F32))
        RC = es.enter_context(nc.sbuf_tensor("RC", [128, 128], F32))
        TN = es.enter_context(nc.sbuf_tensor("TN", [128, 96], F32))
        SMK = es.enter_context(nc.sbuf_tensor("SMK", [128, 1], I32))
        pb = [es.enter_context(nc.psum_tensor(f"pb{i}", [128, 512], F32))
              for i in range(8)]

        AR = AR_t[:]
        AA = AA_t[:]
        AR16 = AR.bitcast(BF16)
        ARI = AR.bitcast(I32)
        ARS = AR.bitcast(I16)

        def arf(o, n):
            return AR[:, o:o + n]

        def arb(o, n):
            return AR16[:, 2 * o:2 * o + n]

        def ari(o, n):
            return ARI[:, o:o + n]

        p = Plan(nc)
        for s in ["qS", "qR", "sP", "sV", "sC", "sG", "fS", "fR", "sY",
                  "qG", "fG"]:
            p.sem(s, es)

        def vsync():
            p.step("vector", lambda e: e.engine_nop(), inc="sY")
            p.wait("vector", "sY")

        QMAP = {"sync": ("qS", "fS"), "scalar": ("qR", "fR"),
                "gpsimd": ("qG", "fG")}

        def qfence(queue):
            qsem, fsem = QMAP[queue]
            p.wait(queue, qsem)
            p.step(queue, lambda e: e.nop(), inc=fsem)
            return p.n(fsem)

        def dbg(name, ap, eng="vector"):
            if not DEBUG or name not in dbg_tensors:
                return
            p.wait("scalar", "sV" if eng == "vector" else
                   ("sC" if eng == "scalar" else "sP"))
            p.dma("scalar", dbg_tensors[name], ap, "qR")

        bank_free = [None] * 8

        def bank_wait(eng, b):
            if bank_free[b] is not None:
                s, v = bank_free[b]
                p.wait(eng, s, v)

        p.dma("sync", IDF[:], idf_h, "qS")
        p.dma("sync", IDB[:], idb_h, "qS")
        p.dma("sync", BZ[:], bz_h, "qS")
        p.dma("sync", BH[:], bh_h, "qS")
        p.dma("sync", SEL[:], sel_h, "qS")
        consts_f = qfence("sync")
        p.wait("tensor", "fS", consts_f)
        p.wait("scalar", "fS", consts_f)
        p.wait("vector", "fS", consts_f)
        p.step("vector", lambda e: e.memset(HST[:], 0.0))
        p.step("vector", lambda e: e.tensor_scalar(
            NBZ[:], BZ[:], -1.0, None, Alu.mult))
        p.step("vector", lambda e: e.memset(AST[:], 1.0))
        p.step("vector", lambda e: e.memset(SMK[:], SIGN_MASK))

        def quant(X, oB, oC, oM, nblk, width, QH=None, QL=None):
            B = arf(oB, width)
            Bi = ari(oB, width)
            C = arf(oC, width)
            Ci = ari(oC, width)
            Mk = ARS[:, 2 * oM:2 * oM + width]
            Xi = X.bitcast(I32)
            B3 = B.rearrange("p (b j) -> p b j", j=16)
            C3 = C.rearrange("p (b j) -> p b j", j=16)
            sc = SC[:, 0:nblk]
            rc = RC[:, 0:nblk]
            sc3 = sc.rearrange("p (b o) -> p b o", o=1)
            rcb = rc.unsqueeze(2).broadcast_to([128, nblk, 16])
            scb = sc.unsqueeze(2).broadcast_to([128, nblk, 16])
            p.step("vector", lambda e: e.tensor_scalar(Bi, Xi, ABS_MASK, None,
                                                       Alu.bitwise_and))
            p.step("vector", lambda e: e.tensor_reduce(sc3, B3, AX, Alu.max))
            vsync()
            p.step("vector", lambda e: e.tensor_scalar(sc, sc, 1.0 / 6.0, 1e-12,
                                                       Alu.mult, Alu.max))
            vsync()
            p.step("vector", lambda e: e.reciprocal(rc, sc))
            vsync()
            p.step("vector", lambda e: e.tensor_tensor(B3, B3, rcb, Alu.mult))
            p.step("vector", lambda e: e.tensor_scalar(Ci, Bi, HALF_DOWN, None,
                                                       Alu.add))
            p.step("vector", lambda e: e.tensor_scalar(Ci, Ci, MASK_HI, None,
                                                       Alu.bitwise_and))
            p.step("vector", lambda e: e.tensor_scalar(Mk, B, 1.0, None,
                                                       Alu.is_lt))
            p.step("vector", lambda e: e.tensor_scalar(B, B, 1.0, None, Alu.add))
            p.step("vector", lambda e: e.tensor_scalar(Bi, Bi, HALF_DOWN, None,
                                                       Alu.add))
            p.step("vector", lambda e: e.tensor_scalar(Bi, Bi, MASK_HI, None,
                                                       Alu.bitwise_and))
            p.step("vector", lambda e: e.tensor_scalar(B, B, -1.0, None, Alu.add))
            p.step("vector", lambda e: e.select(C, Mk, B, C))
            p.step("vector", lambda e: e.tensor_tensor(C3, C3, scb, Alu.mult))
            p.step("vector", lambda e: e.scalar_tensor_tensor(
                Ci, Xi, SMK[:, 0:1], Ci, Alu.bitwise_and, Alu.bitwise_or))
            if QH is not None:
                p.step("vector", lambda e: e.tensor_copy(QH, C))
                p.step("vector", lambda e: e.tensor_tensor(QL, C, QH,
                                                           Alu.subtract))

        def layer_norm(xg):
            T = TN
            M = [T[:, 1:2], T[:, 4:5], T[:, 7:8], T[:, 10:11]]
            C = [T[:, 2:3], T[:, 5:6], T[:, 8:9], T[:, 11:12]]
            p.step("vector", lambda e, x=xg: e.bn_stats(T[:, 0:6], x[:, 0:512]))
            p.step("vector", lambda e, x=xg: e.bn_stats(T[:, 6:12],
                                                        x[:, 512:1024]))
            vsync()
            p.step("vector", lambda e: e.tensor_tensor(T[:, 16:17], M[0], M[1],
                                                       Alu.add))
            p.step("vector", lambda e: e.tensor_tensor(T[:, 17:18], M[2], M[3],
                                                       Alu.add))
            p.step("vector", lambda e: e.tensor_tensor(T[:, 18:19], M[0], M[0],
                                                       Alu.mult))
            p.step("vector", lambda e: e.tensor_tensor(T[:, 19:20], M[1], M[1],
                                                       Alu.mult))
            p.step("vector", lambda e: e.tensor_tensor(T[:, 20:21], M[2], M[2],
                                                       Alu.mult))
            p.step("vector", lambda e: e.tensor_tensor(T[:, 21:22], M[3], M[3],
                                                       Alu.mult))
            p.step("vector", lambda e: e.tensor_tensor(T[:, 22:23], C[0], C[1],
                                                       Alu.add))
            p.step("vector", lambda e: e.tensor_tensor(T[:, 23:24], C[2], C[3],
                                                       Alu.add))
            vsync()
            p.step("vector", lambda e: e.tensor_tensor(
                T[:, 12:13], T[:, 16:17], T[:, 17:18], Alu.add))
            p.step("vector", lambda e: e.tensor_tensor(
                T[:, 13:14], T[:, 22:23], T[:, 23:24], Alu.add))
            p.step("vector", lambda e: e.tensor_tensor(
                T[:, 16:17], T[:, 18:19], T[:, 19:20], Alu.add))
            p.step("vector", lambda e: e.tensor_tensor(
                T[:, 17:18], T[:, 20:21], T[:, 21:22], Alu.add))
            vsync()
            p.step("vector", lambda e: e.tensor_scalar(
                T[:, 12:13], T[:, 12:13], 0.25, None, Alu.mult))
            p.step("vector", lambda e: e.tensor_tensor(
                T[:, 14:15], T[:, 16:17], T[:, 17:18], Alu.add))
            vsync()
            p.step("vector", lambda e: e.tensor_scalar(
                T[:, 16:17], T[:, 14:15], 256.0, None, Alu.mult))
            p.step("vector", lambda e: e.tensor_tensor(
                T[:, 17:18], T[:, 12:13], T[:, 12:13], Alu.mult))
            vsync()
            p.step("vector", lambda e: e.tensor_tensor(
                T[:, 16:17], T[:, 16:17], T[:, 13:14], Alu.add))
            vsync()
            p.step("vector", lambda e: e.tensor_scalar(
                T[:, 15:16], T[:, 16:17], 1.0 / 1024.0, None, Alu.mult))
            vsync()
            p.step("vector", lambda e: e.tensor_tensor(
                T[:, 15:16], T[:, 15:16], T[:, 17:18], Alu.subtract))
            vsync()
            p.step("vector", lambda e: e.tensor_scalar(
                T[:, 14:15], T[:, 15:16], EPS, None, Alu.add), inc="sV")
            p.wait("scalar", "sV")
            p.step("scalar", lambda e: e.activation(
                T[:, 15:16], T[:, 14:15], Act.Sqrt), inc="sC")
            p.wait("vector", "sC")
            p.step("vector", lambda e: e.reciprocal(T[:, 8:9], T[:, 15:16]))
            vsync()
            p.step("vector", lambda e: e.tensor_tensor(
                T[:, 9:10], T[:, 8:9], T[:, 8:9], Alu.mult))
            vsync()
            p.step("vector", lambda e: e.tensor_tensor(
                T[:, 9:10], T[:, 9:10], T[:, 14:15], Alu.mult))
            vsync()
            p.step("vector", lambda e: e.tensor_scalar(
                T[:, 9:10], T[:, 9:10], -0.5, 1.5, Alu.mult, Alu.add))
            vsync()
            p.step("vector", lambda e: e.tensor_tensor(
                T[:, 10:11], T[:, 9:10], T[:, 8:9], Alu.mult))
            p.step("vector", lambda e, x=xg: e.tensor_scalar(
                x, x, T[:, 12:13], None, Alu.subtract))
            p.step("vector", lambda e, x=xg: e.tensor_scalar(
                x, x, T[:, 10:11], None, Alu.mult))

        XSA = arf(O_XG, 1024)
        XSB = arf(O_XG + 1024, 1024)
        XNO = {0: (O_XNH, O_XNL), 1: (12288, 14336),
               2: (O_XNH, O_XNL), 3: (29696, 31744)}
        stats_mark = {}
        xn_ready = {}
        prep_state = {"xsb_free": 0, "ap_mark": 0, "p1f": None,
                      "slot_free": {0: ("sP", 0), 1: ("sP", 0)},
                      "ap_fence": {0: 0, 1: 0}}
        T = TN

        def ln_stats_g(tau, g):
            rows0 = tau * TT + g * 128
            p.wait("gpsimd", "sV")
            if g == 0:
                p.wait("gpsimd", "sP", prep_state["xsb_free"])
            p.dma("gpsimd", XSA, x_h[rows0:rows0 + 128, :], "qG")
            fx = qfence("gpsimd")
            p.wait("vector", "fG", fx)
            b = g * 12
            p.step("vector", lambda e, bb=b: e.bn_stats(
                T[:, bb:bb + 6], XSA[:, 0:512]))
            p.step("vector", lambda e, bb=b: e.bn_stats(
                T[:, bb + 6:bb + 12], XSA[:, 512:1024]), inc="sV")
            stats_mark[(tau, g)] = p.n("sV")

        def _tt(d, a, b, op):
            p.step("vector", lambda e, dd=d, aa=a, bb=b, oo=op: e.tensor_tensor(
                T[:, dd:dd + 1], T[:, aa:aa + 1], T[:, bb:bb + 1], oo))

        def _ts(d, a, s1, s2, op1, op2=None):
            if op2 is None:
                p.step("vector", lambda e, dd=d, aa=a: e.tensor_scalar(
                    T[:, dd:dd + 1], T[:, aa:aa + 1], s1, s2, op1))
            else:
                p.step("vector", lambda e, dd=d, aa=a: e.tensor_scalar(
                    T[:, dd:dd + 1], T[:, aa:aa + 1], s1, s2, op1, op2))

        def ln_agg4():
            vsync()
            for g in range(NG):
                s, u = g * 12, 48 + 8 * g
                _tt(u + 0, s + 1, s + 4, Alu.add)
                _tt(u + 1, s + 7, s + 10, Alu.add)
                _tt(u + 2, s + 1, s + 1, Alu.mult)
                _tt(u + 3, s + 4, s + 4, Alu.mult)
                _tt(u + 4, s + 7, s + 7, Alu.mult)
                _tt(u + 5, s + 10, s + 10, Alu.mult)
                _tt(u + 6, s + 2, s + 5, Alu.add)
                _tt(u + 7, s + 8, s + 11, Alu.add)
            vsync()
            for g in range(NG):
                u = 48 + 8 * g
                _tt(80 + g, u + 0, u + 1, Alu.add)
                _tt(u + 6, u + 6, u + 7, Alu.add)
                _tt(u + 0, u + 2, u + 3, Alu.add)
                _tt(u + 1, u + 4, u + 5, Alu.add)
            vsync()
            for g in range(NG):
                u = 48 + 8 * g
                _ts(80 + g, 80 + g, 0.25, None, Alu.mult, None)
                _tt(u + 2, u + 0, u + 1, Alu.add)
            vsync()
            for g in range(NG):
                u = 48 + 8 * g
                _ts(u + 3, u + 2, 256.0, None, Alu.mult, None)
                _tt(u + 4, 80 + g, 80 + g, Alu.mult)
            vsync()
            for g in range(NG):
                u = 48 + 8 * g
                _tt(u + 3, u + 3, u + 6, Alu.add)
            vsync()
            for g in range(NG):
                u = 48 + 8 * g
                _ts(u + 5, u + 3, 1.0 / 1024.0, None, Alu.mult, None)
            vsync()
            for g in range(NG):
                u = 48 + 8 * g
                _tt(u + 5, u + 5, u + 4, Alu.subtract)
            vsync()
            for g in range(NG):
                u = 48 + 8 * g
                _ts(88 + g, u + 5, EPS, None, Alu.add, None)
            p.step("vector", lambda e: e.engine_nop(), inc="sV")
            p.wait("scalar", "sV")
            p.step("scalar", lambda e: e.activation(
                T[:, 92:96], T[:, 88:92], Act.Sqrt), inc="sC")
            p.wait("vector", "sC")
            p.step("vector", lambda e: e.reciprocal(T[:, 84:88], T[:, 92:96]))
            vsync()
            p.step("vector", lambda e: e.tensor_tensor(
                T[:, 48:52], T[:, 84:88], T[:, 84:88], Alu.mult))
            vsync()
            p.step("vector", lambda e: e.tensor_tensor(
                T[:, 48:52], T[:, 48:52], T[:, 88:92], Alu.mult))
            vsync()
            p.step("vector", lambda e: e.tensor_scalar(
                T[:, 48:52], T[:, 48:52], -0.5, 1.5, Alu.mult, Alu.add))
            vsync()
            p.step("vector", lambda e: e.tensor_tensor(
                T[:, 84:88], T[:, 48:52], T[:, 84:88], Alu.mult))
            vsync()

        def ln_apply_load(tau, g):
            slot = XSB if g % 2 == 0 else XSA
            rows0 = tau * TT + g * 128
            s, v = prep_state["slot_free"][g % 2]
            p.wait("gpsimd", s, v)
            p.dma("gpsimd", slot, x_h[rows0:rows0 + 128, :], "qG")
            prep_state["ap_fence"][g % 2] = qfence("gpsimd")

        def ln_apply_v(tau, g):
            slot = XSB if g % 2 == 0 else XSA
            p.wait("vector", "fG", prep_state["ap_fence"][g % 2])
            p.step("vector", lambda e, x=slot, gg=g: e.tensor_scalar(
                x, x, T[:, 80 + gg:81 + gg], None, Alu.subtract))
            p.step("vector", lambda e, x=slot, gg=g: e.tensor_scalar(
                x, x, T[:, 84 + gg:85 + gg], None, Alu.mult), inc="sV")
            prep_state["ap_mark"] = p.n("sV")
            if tau == 0 and g == 0:
                dbg("xn0", slot)

        def ln_apply_pe(tau, g):
            slot = XSB if g % 2 == 0 else XSA
            b0 = (g % 2) * 2
            p.wait("tensor", "sV", prep_state["ap_mark"])
            bank_wait("tensor", b0)
            bank_wait("tensor", b0 + 1)
            for k in range(NK):
                bb = b0 + k // 4
                cc = (k % 4) * 128
                p.step("tensor", lambda e, b=bb, c=cc, kk=k, x=slot: e.transpose(
                    pb[b][:, c:c + 128], x[:, kk * 128:(kk + 1) * 128],
                    IDF[:]), inc="sP" if k == NK - 1 else None)
            prep_state["xsb_free"] = p.n("sP")
            prep_state["slot_free"][g % 2] = ("sP", p.n("sP"))
            trm = p.n("sP")
            oh, ol = XNO[tau]
            p.wait("vector", "sP", trm)
            if prep_state["p1f"] is not None and tau == 1:
                p.wait("vector", "fR", prep_state["p1f"])
            if tau == 3 and g == 0:
                p.wait("vector", "fR", p2_state["fence"])
            for k in range(NK):
                bb = b0 + k // 4
                cc = (k % 4) * 128
                xnh = arb(oh + k * 256, 512)
                xnl = arb(ol + k * 256, 512)
                p.step("vector", lambda e, o=xnh, b=bb, c=cc, gg=g:
                       e.tensor_copy(o[:, gg * 128:(gg + 1) * 128],
                                     pb[b][:, c:c + 128]))
                p.step("vector", lambda e, o=xnl, b=bb, c=cc, gg=g, hh=xnh:
                       e.tensor_tensor(o[:, gg * 128:(gg + 1) * 128],
                                       pb[b][:, c:c + 128],
                                       hh[:, gg * 128:(gg + 1) * 128],
                                       Alu.subtract),
                       inc="sV" if k == NK - 1 else None)
            em = p.n("sV")
            bank_free[b0] = ("sV", em)
            bank_free[b0 + 1] = ("sV", em)
            if g == NG - 1:
                xn_ready[tau] = em

        def prep_step(tau, s):
            if s < NG:
                ln_stats_g(tau, s)
                if s == NG - 1:
                    prep_state["slot_free"][1] = ("sV", stats_mark[(tau, s)])
                    ln_agg4()
                    ln_apply_load(tau, 0)
            elif s == NG:
                ln_apply_v(tau, 0)
                ln_apply_load(tau, 1)
            elif s < NK - 1:
                g = s - NG
                ln_apply_pe(tau, g - 1)
                ln_apply_v(tau, g)
                ln_apply_load(tau, g + 1)
            else:
                ln_apply_pe(tau, 2)
                ln_apply_v(tau, 3)
                ln_apply_pe(tau, 3)

        PB1 = 8192
        p1_slot_fence = {}
        p1_chunks = [(wz_h, wzh_s, wzl_s, k) for k in range(NK)] +                     [(wh_h, whh_s, whl_s, k) for k in range(NK)]
        for j, (w_in, s_hi, s_lo, k) in enumerate(p1_chunks):
            slot = j % 4
            a_in = arf(PB1 + slot * 2048, 1024)
            a_hi = arb(PB1 + slot * 2048 + 1024, 1024)
            a_lo = arb(PB1 + slot * 2048 + 1536, 1024)
            if j >= 4:
                p.wait("sync", "fR", p1_slot_fence[slot])
            p.dma("sync", a_in, w_in[k * 128:(k + 1) * 128, :], "qS")
            fv = qfence("sync")
            p.wait("vector", "fS", fv)
            p.step("vector", lambda e, o=a_hi, i=a_in: e.tensor_copy(o, i))
            p.step("vector", lambda e, o=a_lo, i=a_in, h=a_hi:
                   e.tensor_tensor(o, i, h, Alu.subtract), inc="sV")
            p.wait("scalar", "sV")
            p.dma("scalar", s_hi[k * 128:(k + 1) * 128, :], a_hi, "qR")
            p.dma("scalar", s_lo[k * 128:(k + 1) * 128, :], a_lo, "qR")
            p1_slot_fence[slot] = qfence("scalar")
            if j % 2 == 1 and j // 2 < 8:
                prep_step(0, j // 2)
        p1_out_f = qfence("scalar")
        p.wait("sync", "fR", p1_out_f)
        prep_state["p1f"] = p1_out_f

        def wz_piece(roll, idx):
            o = O_WZ + (roll * 4 + idx) * 512
            return arb(o, 1024).rearrange("p (k j) -> p k j", j=128)

        mixer_scratch = [wzh_s, wzl_s, whh_s, whl_s]
        wz_piece_done = {}
        wz_buf_free = {}
        p2_jobs = [("d", ck) for ck in range(8)] + [("u", u) for u in range(6)]
        p2_keys = [(0, m) for m in range(NK)] + [(1, m) for m in range(NK - 1)]
        p2_state = {"fence": p1_out_f, "adone": {}}

        def p2_stageA(j):
            kind, arg = p2_jobs[j]
            w = 1024 if kind == "u" else 384
            a_in = arf(P2A, w)
            p.wait("gpsimd", "sV")
            if kind == "u":
                p.dma("gpsimd", a_in, ws1_h[arg * 128:(arg + 1) * 128, :], "qG")
            else:
                p.dma("gpsimd", a_in, ws2_h[arg * 128:(arg + 1) * 128, :], "qG")
            fp2 = qfence("gpsimd")
            p.wait("vector", "fG", fp2)
            quant(a_in, P2B, P2C, P2M, w // 16, w, arb(P2E, w), arb(P2F, w))
            p.step("vector", lambda e: e.engine_nop(), inc="sV")
            p2_state["adone"][j] = p.n("sV")

        def p2_stageB(j):
            kind, arg = p2_jobs[j]
            p.wait("tensor", "sV", p2_state["adone"][j])
            if kind == "u":
                bank_wait("tensor", 0)
                bank_wait("tensor", 1)
                for hl, so in ((0, P2E), (1, P2F)):
                    for k in range(NK):
                        p.step("tensor", lambda e, bb=hl, oo=so, kk=k:
                               e.transpose(
                            pb[bb][:].bitcast(BF16)[:, kk * 128:(kk + 1) * 128],
                            arb(oo, 1024)[:, kk * 128:(kk + 1) * 128], IDB[:]),
                            inc="sP" if (hl == 1 and k == NK - 1) else None)
                trc = p.n("sP")
                p.wait("vector", "sP", trc)
                if p2_state["fence"] is not None:
                    p.wait("vector", "fR", p2_state["fence"])
                p.step("vector", lambda e: e.tensor_copy(
                    arb(8192, 1024), pb[0][:].bitcast(BF16)[:, 0:1024]))
                p.step("vector", lambda e: e.tensor_copy(
                    arb(8704, 1024), pb[1][:].bitcast(BF16)[:, 0:1024]),
                    inc="sV")
                evc = p.n("sV")
                bank_free[0] = ("sV", evc)
                bank_free[1] = ("sV", evc)
                p.wait("scalar", "sV", evc)
                agt = ag1a_in if arg < 3 else ag1b_in
                for hl, eo in ((0, 8192), (1, 8704)):
                    col = (2 * (arg % 3) + hl) * 128
                    p.dma("scalar",
                          agt[:, col:col + 128].rearrange(
                              "(k p) j -> p k j", p=128),
                          arb(eo, 1024).rearrange(
                              "p (k j) -> p k j", j=128), "qR")
            else:
                bank_wait("tensor", 0)
                for hl, so in ((0, P2E), (1, P2F)):
                    for lb in range(3):
                        col = (hl * 3 + lb) * 128
                        p.step("tensor", lambda e, cc=col, oo=so, ll=lb:
                               e.transpose(
                            pb[0][:].bitcast(BF16)[:, cc:cc + 128],
                            arb(oo, 384)[:, ll * 128:(ll + 1) * 128], IDB[:]),
                            inc="sP" if (hl == 1 and lb == 2) else None)
                trc = p.n("sP")
                p.wait("vector", "sP", trc)
                if p2_state["fence"] is not None:
                    p.wait("vector", "fR", p2_state["fence"])
                p.step("vector", lambda e: e.tensor_copy(
                    arb(8192, 768), pb[0][:].bitcast(BF16)[:, 0:768]),
                    inc="sV")
                evc = p.n("sV")
                bank_free[0] = ("sV", evc)
                p.wait("scalar", "sV", evc)
                for hl in (0, 1):
                    for lb in range(3):
                        p.dma("scalar",
                              ag2_in[lb * 128:(lb + 1) * 128,
                                     hl * 1024 + arg * 128:
                                     hl * 1024 + (arg + 1) * 128],
                              arb(8192 + (hl * 3 + lb) * 64, 128), "qR")
            p2_state["fence"] = qfence("scalar")

        def p2_emit(key):
            i = p2_keys.index(key)
            if i > 0 and i - 1 < len(p2_jobs):
                p2_stageB(i - 1)
                trig = {7: (ag2_in, ag2_out), 10: (ag1a_in, ag1a_out),
                        13: (ag1b_in, ag1b_out)}.get(i - 1)
                if trig is not None:
                    p.wait("gpsimd", "fR", p2_state["fence"])
                    p.step("gpsimd", lambda e, a=trig[0], b=trig[1]:
                           e.collective_compute("AllGather", Alu.bypass,
                                                replica_groups=RG,
                                                ins=[a], outs=[b]), inc="sG")
            if i < len(p2_jobs):
                p2_stageA(i)
        h_written = {}

        for tau in range(NT):
            if tau == 0:
                dbg("xnh", arb(O_XNH, 4096))
                dbg("xnl", arb(O_XNL, 4096))

            def wz_fetch(tau_, m_):
                rollm_ = (tau_ * NK + m_) % 2
                for idx4, scr in enumerate(mixer_scratch):
                    key = (rollm_, idx4)
                    if key in wz_buf_free:
                        s, v = wz_buf_free[key]
                        p.wait("sync", s, v)
                    dst = wz_piece(rollm_, idx4)
                    srcw = scr.rearrange("(k q) (mm j) -> q k mm j",
                                         q=128, j=128)[:, :, m_, :]
                    p.dma("sync", dst, srcw, "qS")
                wz_piece_done[(tau_, m_)] = qfence("sync")
            if tau == 0:
                wz_fetch(0, 0)
            for m in range(NK):
                rollm = (tau * NK + m) % 2
                if m < NK - 1:
                    wz_fetch(tau, m + 1)
                elif tau < NT - 1:
                    wz_fetch(tau + 1, 0)
                z_done = h_done = None
                for mat in range(2):
                    bank = 4 + (2 * m + mat) % 4
                    bank_wait("tensor", bank)
                    p.wait("tensor", "fS", wz_piece_done[(tau, m)])
                    if m == 0 and mat == 0:
                        p.wait("tensor", "sV", xn_ready[tau])
                    hi_p = wz_piece(rollm, 2 * mat + 0)
                    lo_p = wz_piece(rollm, 2 * mat + 1)
                    xno_h, xno_l = XNO[tau]
                    for k in range(NK):
                        xnh = arb(xno_h + k * 256, 512)
                        xnl = arb(xno_l + k * 256, 512)
                        p.step("tensor", lambda e, b=bank, hp=hi_p, kk=k, r=xnh,
                               s0=(k == 0): e.matmul(
                            pb[b][:, 0:512], hp[:, kk, :], r,
                            start=s0, stop=False))
                        p.step("tensor", lambda e, b=bank, hp=hi_p, kk=k, r=xnl:
                               e.matmul(pb[b][:, 0:512], hp[:, kk, :], r,
                                        start=False, stop=False))
                        p.step("tensor", lambda e, b=bank, lp=lo_p, kk=k, r=xnh,
                               s1=(k == NK - 1): e.matmul(
                            pb[b][:, 0:512], lp[:, kk, :], r,
                            start=False, stop=s1),
                            inc="sP" if k == NK - 1 else None)
                    mm_done = p.n("sP")
                    p.wait("scalar", "sP", mm_done)
                    if mat == 0:
                        zb = arf(O_ZB + (m % 2) * 512, 512)
                        acol0 = AA[:, tau * 4096 + m * 512:
                                   tau * 4096 + (m + 1) * 512]
                        p.step("scalar", lambda e, o=zb, b=bank, mm=m:
                               e.activation(o, pb[b][:, 0:512], Act.Sigmoid,
                                            bias=BZ[:, mm:mm + 1]))
                        p.step("scalar", lambda e, o=acol0, b=bank, mm=m:
                               e.activation(o, pb[b][:, 0:512], Act.Sigmoid,
                                            bias=NBZ[:, mm:mm + 1],
                                            scale=-1.0), inc="sC")
                        z_done = p.n("sC")
                    else:
                        hb = arf(O_HT + (m % 2) * 512, 512)
                        p.step("scalar", lambda e, o=hb, b=bank, mm=m: e.add(
                            o, pb[b][:, 0:512], BH[:, mm:mm + 1]), inc="sC")
                        h_done = p.n("sC")
                    bank_free[bank] = ("sC", p.n("sC"))
                for idx4 in range(4):
                    wz_buf_free[(rollm, idx4)] = ("sP", mm_done)
                p.wait("vector", "sC", h_done)
                zb = arf(O_ZB + (m % 2) * 512, 512)
                hb = arf(O_HT + (m % 2) * 512, 512)
                acol = AA[:, tau * 4096 + m * 512: tau * 4096 + (m + 1) * 512]
                hcol = arf(O_H + tau * 4096 + m * 512, 512)
                bb = arf(O_BB, 512)
                cu = arf(O_CU, 512)
                if tau == 0 and m == 0:
                    dbg("z0", zb, eng="scalar")
                    dbg("ht0", hb, eng="scalar")
                if tau == 2 and m == 0:
                    p.wait("vector", "fR", p2_state["fence"])
                p.step("vector", lambda e, z=zb, h=hb, b=bb:
                       e.tensor_tensor(b, z, h, Alu.mult))
                if m > 0:
                    p.step("vector", lambda e, o=cu, mm=m - 1: e.tensor_copy(
                        AST[:, mm:mm + 1], o[:, 511:512]))
                p.step("vector", lambda e, o=hcol, a=acol, b=bb, mm=m:
                       e.tensor_tensor_scan(o, a, b, HST[:, mm:mm + 1],
                                            Alu.mult, Alu.add))
                p.step("vector", lambda e, o=cu, a=acol, mm=m:
                       e.tensor_tensor_scan(o, a, a, AST[:, mm:mm + 1],
                                            Alu.mult, Alu.bypass))
                p.step("vector", lambda e, o=hcol, mm=m: e.tensor_copy(
                    HST[:, mm:mm + 1], o[:, 511:512]), inc="sV")
                if (tau, m) in p2_keys:
                    p2_emit((tau, m))
                if tau < NT - 1:
                    prep_step(tau + 1, m)
            vsync()
            p.step("vector", lambda e: e.tensor_copy(
                AST[:, 7:8], arf(O_CU + 511, 1)), inc="sV")
            h_written[tau] = p.n("sV")

            if False:
                for kind, arg in p2_sched[tau]:
                    if kind == "u":
                        u = arg
                        a_in = arf(P2A, 1024)
                        p.wait("sync", "sV")
                        p.dma("sync", a_in, ws1_h[u * 128:(u + 1) * 128, :], "qS")
                        fp2 = qfence("sync")
                        p.wait("vector", "fS", fp2)
                        quant(a_in, P2B, P2C, P2M, 64, 1024,
                              arb(P2E, 1024), arb(P2F, 1024))
                        p.step("vector", lambda e: e.engine_nop(), inc="sV")
                        p.wait("scalar", "sV")
                        p.dma("scalar", ag1_in[u * 128:(u + 1) * 128, 0:1024],
                              arb(P2E, 1024), "qR")
                        p.dma("scalar", ag1_in[u * 128:(u + 1) * 128, 1024:2048],
                              arb(P2F, 1024), "qR")
                    else:
                        c0, w = arg
                        a_in = arf(P2A, w)
                        p.wait("sync", "sV")
                        p.dma("sync", a_in, ws2_h[:, c0:c0 + w], "qS")
                        fp2 = qfence("sync")
                        p.wait("vector", "fS", fp2)
                        quant(a_in, P2B, P2C, P2M, w // 16, w,
                              arb(P2E, w), arb(P2F, w))
                        p.step("vector", lambda e: e.engine_nop(), inc="sV")
                        p.wait("scalar", "sV")
                        p.dma("scalar", ag2_in[:, c0:c0 + w], arb(P2E, w), "qR")
                        p.dma("scalar", ag2_in[:, DI + c0:DI + c0 + w],
                              arb(P2F, w), "qR")
        dbg("a_all", AA)
        dbg("h_allM", arf(O_H, 16384))
        dbg("ast", AST[:])
        dbg("hst", HST[:])
        agw_done = 3

        p.wait("sync", "sP", prep_state["xsb_free"])
        p.dma("sync", arf(O_D, 1024), ldt_h, "qS")
        fld = qfence("sync")
        p.wait("scalar", "fS", fld)
        p.step("scalar", lambda e: e.activation(
            arf(O_D, 1024), arf(O_D, 1024), Act.Exp), inc="sC")
        d_ready = p.n("sC")

        p.wait("tensor", "sV", h_written[NT - 1])
        bank_wait("tensor", 4)
        p.step("tensor", lambda e: e.transpose(
            pb[4][0:8, 0:128], AST[:], IDF[:]))
        p.step("tensor", lambda e: e.transpose(
            pb[4][0:8, 128:256], HST[:], IDF[:]), inc="sP")
        p.wait("vector", "sP")
        p.step("vector", lambda e: e.tensor_copy(
            AR_t[0:8, O_GAT:O_GAT + 256], pb[4][0:8, 0:256]), inc="sV")
        bank_free[4] = ("sV", p.n("sV"))
        p.wait("gpsimd", "sV")
        p.dma("gpsimd", ccm_in[0:1, :].rearrange("a (c q) -> (a c) q", q=128),
              AR_t[0:8, O_GAT:O_GAT + 128], "qG")
        p.dma("gpsimd", ccm_in[1:2, :].rearrange("a (c q) -> (a c) q", q=128),
              AR_t[0:8, O_GAT + 128:O_GAT + 256], "qG")
        p.wait("gpsimd", "qG")
        p.step("gpsimd", lambda e: e.collective_compute(
            "AllReduce" if False else "AllGather", Alu.bypass,
            replica_groups=RG, ins=[ccm_in], outs=[ccm_out]), inc="sG")
        p.wait("gpsimd", "sG")
        p.dma("gpsimd", AR_t[0:16, O_GAT:O_GAT + 1024], ccm_out, "qG")
        fgat = qfence("gpsimd")
        p.wait("tensor", "fG", fgat)
        bank_wait("tensor", 0)
        for c in range(8):
            p.step("tensor", lambda e, cc=c: e.transpose(
                pb[0][:, cc * 16:(cc + 1) * 16],
                AR_t[0:16, O_GAT + cc * 128:O_GAT + (cc + 1) * 128],
                IDF[0:16, 0:16]), inc="sP" if c == 7 else None)
        p.wait("vector", "sP")
        SEQ = arf(O_SEQ, 128)
        p.step("vector", lambda e: e.tensor_copy(SEQ, pb[0][:, 0:128]),
               inc="sV")
        bank_free[0] = ("sV", p.n("sV"))
        vsync()
        PRF = arf(O_PRF, 64)
        for c in range(8):
            s3 = SEQ[:, c * 16:(c + 1) * 16].rearrange(
                "p (r two) -> p r two", two=2)
            p.step("vector", lambda e, cc=c, a=s3: e.tensor_tensor_scan(
                PRF[:, cc * 8:(cc + 1) * 8], a[:, :, 0:1].squeeze(2),
                a[:, :, 1:2].squeeze(2), 0.0, Alu.mult, Alu.add))
        vsync()
        p.step("vector", lambda e: e.tensor_tensor(
            arf(O_GAT, 64), PRF, SEL[:], Alu.mult))
        vsync()
        p.step("vector", lambda e: e.tensor_reduce(
            FST[:].rearrange("p (c o) -> p c o", o=1),
            arf(O_GAT, 64).rearrange("p (c r) -> p c r", r=8), AX, Alu.add),
            inc="sV")
        vsync()
        dbg("seq", SEQ)
        dbg("prf", PRF)
        dbg("fst", FST[:])

        p.wait("vector", "sC", d_ready)

        xg_used = {}
        for tau in range(NT):
            for m in range(NK):
                fb = arf(O_F + (m % 2) * 512, 512)
                acol = AA[:, tau * 4096 + m * 512: tau * 4096 + (m + 1) * 512]
                hcol = arf(O_H + tau * 4096 + m * 512, 512)
                p.step("vector", lambda e, o=fb, a=acol, mm=m:
                       e.tensor_tensor_scan(o, a, a, FST[:, mm:mm + 1],
                                            Alu.mult, Alu.bypass))
                p.step("vector", lambda e, h=hcol, f=fb: e.tensor_tensor(
                    h, h, f, Alu.add))
                p.step("vector", lambda e, o=fb, mm=m: e.tensor_copy(
                    FST[:, mm:mm + 1], o[:, 511:512]), inc="sV")
                if tau == 0 and m == 0:
                    dbg("fb0", fb)
            hf_done = p.n("sV")

            p.wait("tensor", "sV", hf_done)
            for g in range(NG):
                idx = tau * NG + g
                slot = g % 2
                xx = arf(O_XXG + slot * 1024, 1024)
                if idx >= 2:
                    p.wait("gpsimd", "sV", xg_used[idx - 2])
                rows0 = tau * TT + g * 128
                p.dma("gpsimd", xx, x_h[rows0:rows0 + 128, :], "qG")
                xload = qfence("gpsimd")
                b0, b1 = 2 * (g % 2), 2 * (g % 2) + 1
                bank_wait("tensor", b0)
                bank_wait("tensor", b1)
                for k in range(NK):
                    b = b0 if k < 4 else b1
                    p.step("tensor", lambda e, bb=b, kk=k, gg=g, t=tau:
                           e.transpose(
                        pb[bb][:, (kk % 4) * 128:(kk % 4 + 1) * 128],
                        arf(O_H + t * 4096 + kk * 512 + gg * 128, 128),
                        IDF[:]), inc="sP" if k == NK - 1 else None)
                trg = p.n("sP")
                slot = g % 2
                htm = arf(O_HTM + slot * 1024, 1024)
                xx = arf(O_XXG + slot * 1024, 1024)
                ycol = AA[:, tau * 4096 + g * 1024: tau * 4096 + (g + 1) * 1024]
                p.wait("vector", "sP", trg)
                p.step("vector", lambda e, o=htm, bb=b1: e.tensor_copy(
                    o[:, 512:1024], pb[bb][:, 0:512]))
                p.step("vector", lambda e, o=htm, bb=b0: e.tensor_copy(
                    o[:, 0:512], pb[bb][:, 0:512]), inc="sV")
                if tau == 0 and g == 0:
                    dbg("htm0", htm)
                bank_free[b0] = ("sV", p.n("sV"))
                bank_free[b1] = ("sV", p.n("sV"))
                p.wait("vector", "fG", xload)
                p.step("vector", lambda e, y=ycol, h=htm: e.tensor_tensor(
                    y[:, 0:1], h[:, 1023:1024], h[:, 1:2], Alu.add))
                p.step("vector", lambda e, y=ycol, h=htm: e.tensor_tensor(
                    y[:, 1023:1024], h[:, 1022:1023], h[:, 0:1], Alu.add))
                p.step("vector", lambda e, y=ycol, h=htm: e.tensor_tensor(
                    y[:, 1:1023], h[:, 0:1022], h[:, 2:1024], Alu.add))
                p.step("vector", lambda e, y=ycol, h=htm:
                       e.scalar_tensor_tensor(y, h, -2.0, y, Alu.mult, Alu.add))
                p.step("vector", lambda e, y=ycol: e.tensor_tensor(
                    y, y, arf(O_D, 1024), Alu.mult))
                p.step("vector", lambda e, y=ycol, h=htm: e.tensor_tensor(
                    y, y, h, Alu.add))
                p.step("vector", lambda e, y=ycol, x=xx: e.tensor_tensor(
                    y, y, x, Alu.add), inc="sV")
                xg_used[idx] = p.n("sV")

        dbg("h_allX", arf(O_H, 16384))
        dbg("yprov", AA)
        if DEBUG:
            fdbg = qfence("scalar")
            p.wait("vector", "fR", fdbg)
            p.wait("sync", "fR", fdbg)

        def wg_piece_ap(slot, W):
            return arb(O_WG + slot * 2048, 8 * W).rearrange(
                "p (k j) -> p k j", j=W)

        hq_state = {"trc": 0, "evc": 0}

        def emit_hq_tr(pending):
            n0, W, qcnt = pending
            nic = W // 128
            npair = 2 * nic
            p.wait("tensor", "sV", qcnt)
            for pr in range(npair):
                hl, ic = pr // nic, pr % nic
                bank = 4 + pr // 2
                if pr % 2 == 0:
                    bank_wait("tensor", bank)
                for g in range(NG):
                    srct = arb(O_HQS + hl * 1024 + g * (W // 2), W)[
                        :, ic * 128:(ic + 1) * 128]
                    p.step("tensor", lambda e, b=bank, s=srct, gg=g,
                           c=(pr % 2) * 512: e.transpose(
                        pb[b][:].bitcast(BF16)[:, c + gg * 128:c + (gg + 1) * 128],
                        s, IDB[:]),
                        inc="sP" if (pr == npair - 1 and g == NG - 1) else None)
            trc = p.n("sP")
            hq_state["trc"] = trc
            p.wait("scalar", "sP", trc)
            for pr in range(npair):
                hl, ic = pr // nic, pr % nic
                bank = 4 + pr // 2
                ig = n0 // 128 + ic
                dsto = (O_HQTH if hl == 0 else O_HQTL) + ig * 256
                p.step("scalar", lambda e, o=arb(dsto, 512), b=bank,
                       c=(pr % 2) * 512: e.activation(
                    o, pb[b][:].bitcast(BF16)[:, c:c + 512], Act.Copy),
                    inc="sC")
            for j in range(4):
                bank_free[4 + j] = ("sC", p.n("sC"))
            hq_state["evc"] = p.n("sC")

        def f_apply_v(t1, g):
            rows0 = t1 * TT + g * 128
            p.wait("scalar", "sV")
            p.dma("scalar", XSB, x_h[rows0:rows0 + 128, :], "qR")
            fx = qfence("scalar")
            p.wait("vector", "fR", fx)
            p.step("vector", lambda e, gg=g: e.tensor_scalar(
                XSB, XSB, T[:, 80 + gg:81 + gg], None, Alu.subtract))
            p.step("vector", lambda e, gg=g: e.tensor_scalar(
                XSB, XSB, T[:, 84 + gg:85 + gg], None, Alu.mult), inc="sV")

        def f_prep_step(t1, s):
            if s < NG:
                ln_stats_g(t1, s)
                if s == NG - 1:
                    ln_agg4()
            else:
                g = s - NG
                f_apply_v(t1, g)
                quant(XSB, 4096, g * 1024, 5120, 64, 1024)
                p.step("vector", lambda e: e.engine_nop(), inc="sV")

        f_stash_ready = {}
        FPREP_SCHED = {1: 0, 3: 1, 5: 2, 7: 3, 9: 4, 12: 5, 15: 6, 18: 7}

        p.wait("sync", "sG", agw_done)
        if DEBUG:
            p.wait("scalar", "sG", agw_done)
            p.dma("scalar", dbg_tensors["agin1"][:, 0:768], ag1a_in, "qR")
            p.dma("scalar", dbg_tensors["agin1"][:, 768:1536], ag1b_in, "qR")
            p.dma("scalar", dbg_tensors["agout1"][:, 0:768], ag1a_out, "qR")
            p.dma("scalar", dbg_tensors["agout1"][:, 768:1536], ag1b_out, "qR")
            p.dma("scalar", dbg_tensors["agin2"], ag2_in, "qR")
            p.dma("scalar", dbg_tensors["agout2"], ag2_out, "qR")
        p.wait("sync", "sV")
        p.wait("scalar", "sV")

        fxg_free = {}
        wg_slot_free = {}
        store_done = {}
        for tau in range(NT):
            xq_ev = {}
            for g in range(NG):
                if tau == 0:
                    idx = tau * NG + g
                    slot = g % 2
                    xg = arf(O_FXG + slot * 1024, 1024)
                    if idx >= 2:
                        p.wait("scalar", "sP", fxg_free[idx - 2])
                    rows0 = tau * TT + g * 128
                    p.dma("scalar", xg, x_h[rows0:rows0 + 128, :], "qR")
                    fx = qfence("scalar")
                    p.wait("vector", "fR", fx)
                    layer_norm(xg)
                    quant(xg, O_S1, O_S2, O_S3, 64, 1024)
                    p.step("vector", lambda e, x=xg: e.tensor_copy(
                        x, arf(O_S2, 1024)), inc="sV")
                    p.wait("tensor", "sV", p.n("sV"))
                else:
                    xg = arf(g * 1024, 1024)
                    if g == 0:
                        p.wait("tensor", "sV", f_stash_ready[tau])
                if g > 0:
                    p.wait("tensor", "sV", xq_ev[g - 1])
                for k in range(NK):
                    if g == 0:
                        bank_wait("tensor", k)
                    p.step("tensor", lambda e, b=k, gg=g, x=xg: e.transpose(
                        pb[b][:, gg * 128:(gg + 1) * 128],
                        x[:, b * 128:(b + 1) * 128], IDF[:]),
                        inc="sP" if k == NK - 1 else None)
                fxg_free[tau * NG + g] = p.n("sP")
                p.wait("vector", "sP", fxg_free[tau * NG + g])
                for k in range(NK):
                    xqh = arb(O_XQH + k * 256 + g * 64, 128)
                    xql = arb(O_XQL + k * 256 + g * 64, 128)
                    p.step("vector", lambda e, o=xqh, b=k, gg=g: e.tensor_copy(
                        o, pb[b][:, gg * 128:(gg + 1) * 128]))
                    p.step("vector", lambda e, o=xql, b=k, gg=g, h=xqh:
                           e.tensor_tensor(o, pb[b][:, gg * 128:(gg + 1) * 128],
                                           h, Alu.subtract),
                           inc="sV" if k == NK - 1 else None)
                xq_ev[g] = p.n("sV")
                if g == NG - 1:
                    for k in range(NK):
                        bank_free[k] = ("sV", xq_ev[g])

            pending = None
            for ni, (n0, W) in enumerate(NSL):
                for mat in range(2):
                    sA, sB = 2 * mat, 2 * mat + 1
                    for sslot, hl in ((sA, 0), (sB, 1)):
                        if sslot in wg_slot_free:
                            p.wait("sync", "sP", wg_slot_free[sslot])
                        dst = wg_piece_ap(sslot, W)
                        for uu in range(W // 128):
                            ju = 22 * mat + n0 // 128 + uu
                            cj, lj = ju // 6, ju % 6
                            agt = ag1a_out if lj < 3 else ag1b_out
                            col = (2 * (lj % 3) + hl) * 128
                            src = agt[cj * D:(cj + 1) * D,
                                      col:col + 128].rearrange(
                                "(k q) j -> q k j", q=128)
                            p.dma("sync", dst[:, :, uu * 128:(uu + 1) * 128],
                                  src, "qS")
                    piece_f = qfence("sync")
                    if DEBUG and tau == 0 and ni == 0 and mat == 0:
                        p.wait("scalar", "fS", piece_f)
                        p.dma("scalar", dbg_tensors["piece0"],
                              arb(O_WG, 4096), "qR")
                    p.wait("tensor", "fS", piece_f)
                    hi_pc = wg_piece_ap(sA, W)
                    lo_pc = wg_piece_ap(sB, W)
                    g_stop = {}
                    for g in range(NG):
                        bank = 4 * mat + g
                        bank_wait("tensor", bank)
                        if ni == 0 and mat == 0:
                            p.wait("tensor", "sV", xq_ev[g])
                        for ph, (so, pc) in enumerate(
                                [(O_XQH, hi_pc), (O_XQL, hi_pc),
                                 (O_XQH, lo_pc)]):
                            for k in range(NK):
                                stat = arb(so + k * 256 + g * 64, 128)
                                p.step("tensor", lambda e, b=bank, st=stat,
                                       mv=pc, kk=k, ww=W,
                                       s0=(ph == 0 and k == 0),
                                       s1=(ph == 2 and k == NK - 1): e.matmul(
                                    pb[b][:, 0:ww], st, mv[:, kk, :],
                                    start=s0, stop=s1),
                                    inc="sP" if (ph == 2 and k == NK - 1)
                                    else None)
                        g_stop[g] = p.n("sP")
                    wg_slot_free[sA] = p.n("sP")
                    wg_slot_free[sB] = p.n("sP")
                    if mat == 0:
                        for g in range(NG):
                            gs = arf(O_GS + g * 512, W)
                            p.wait("scalar", "sP", g_stop[g])
                            p.step("scalar", lambda e, o=gs, b=g, ww=W:
                                   e.activation(o, pb[b][:, 0:ww], Act.Silu),
                                   inc="sC")
                            bank_free[g] = ("sC", p.n("sC"))
                    else:
                        for g in range(NG):
                            gs = arf(O_GS + g * 512, W)
                            h2 = arf(O_H2 + g * 512, W)
                            p.wait("vector", "sP", g_stop[g])
                            p.wait("vector", "sC")
                            p.step("vector", lambda e, o=h2, b=4 + g, s=gs,
                                   ww=W: e.tensor_tensor(
                                o, pb[b][:, 0:ww], s, Alu.mult),
                                inc="sV")
                            bank_free[4 + g] = ("sV", p.n("sV"))
                if pending is not None:
                    emit_hq_tr(pending)
                    pending = None
                p.wait("vector", "sP", hq_state["trc"])
                if W == 512:
                    quant(arf(O_H2, 2048), O_FXG, O_S1, O_S3, 128, 2048,
                          arb(O_HQS, 2048), arb(O_HQS + 1024, 2048))
                else:
                    for g in range(NG):
                        p.step("vector", lambda e, gg=g: e.tensor_copy(
                            arf(O_S2 + gg * 256, 256),
                            arf(O_H2 + gg * 512, 256)))
                    quant(arf(O_S2, 1024), O_FXG, O_S1, O_S3, 64, 1024,
                          arb(O_HQS, 1024), arb(O_HQS + 1024, 1024))
                p.step("vector", lambda e: e.engine_nop(), inc="sV")
                pending = (n0, W, p.n("sV"))
            emit_hq_tr(pending)
            pending = None
            hqT_done = p.n("sC")

            wd_free = {}
            for i in range(NI):
                roll = i % 2
                if i >= 2:
                    p.wait("sync", "sP", wd_free[i - 2])
                if i == 0:
                    p.wait("sync", "sC", hqT_done)
                dsth = arb(O_WD + roll * 1024, 1024)
                dstl = arb(O_WD + roll * 1024 + 512, 1024)
                cc, lb = (i // 3, i % 3) if i < 18 else \
                    (6 + (i - 18) // 2, (i - 18) % 2)
                r0 = cc * 384 + lb * 128
                p.dma("sync", dsth, ag2_out[r0:r0 + 128, 0:1024], "qS")
                p.dma("sync", dstl, ag2_out[r0:r0 + 128, 1024:2048], "qS")
                dwd = qfence("sync")
                if DEBUG and tau == 0 and i == 0:
                    p.wait("scalar", "fS", dwd)
                    p.dma("scalar", dbg_tensors["wd0"], arb(O_WD, 2048), "qR")
                p.wait("tensor", "fS", dwd)
                if i == 0:
                    for b in range(8):
                        bank_wait("tensor", b)
                mh = dsth.rearrange("p (a j) -> p a j", a=2)
                ml = dstl.rearrange("p (a j) -> p a j", a=2)
                for g in range(NG):
                    sth = arb(O_HQTH + i * 256, 512)[:, g * 128:(g + 1) * 128]
                    stl = arb(O_HQTL + i * 256, 512)[:, g * 128:(g + 1) * 128]
                    for half in range(2):
                        b = 2 * g + half
                        last = (i == NI - 1)
                        p.step("tensor", lambda e, bb=b, s=sth, m=mh, hh=half,
                               s0=(i == 0): e.matmul(
                            pb[bb][:, 0:512], s, m[:, hh, :],
                            start=s0, stop=False))
                        p.step("tensor", lambda e, bb=b, s=sth, m=ml, hh=half:
                               e.matmul(pb[bb][:, 0:512], s, m[:, hh, :],
                                        start=False, stop=False))
                        p.step("tensor", lambda e, bb=b, s=stl, m=mh, hh=half,
                               s1=last: e.matmul(
                            pb[bb][:, 0:512], s, m[:, hh, :],
                            start=False, stop=s1),
                            inc="sP" if (g == NG - 1 and half == 1) else None)
                wd_free[i] = p.n("sP")
                if tau < NT - 1 and i in FPREP_SCHED:
                    f_prep_step(tau + 1, FPREP_SCHED[i])
                    if FPREP_SCHED[i] == 7:
                        f_stash_ready[tau + 1] = p.n("sV")
            down_done = p.n("sP")

            p.wait("vector", "sP", down_done)
            for g in range(NG):
                idx = tau * NG + g
                ybuf = arf(O_WG + (g % 2) * 1024, 1024)
                if idx >= 2:
                    p.wait("vector", "fR", store_done[idx - 2])
                ycol = AA[:, tau * 4096 + g * 1024: tau * 4096 + (g + 1) * 1024]
                p.step("vector", lambda e, o=ybuf, b=2 * g, y=ycol:
                       e.tensor_tensor(o[:, 0:512], pb[b][:, 0:512],
                                       y[:, 0:512], Alu.add))
                p.step("vector", lambda e, o=ybuf, b=2 * g + 1, y=ycol:
                       e.tensor_tensor(o[:, 512:1024], pb[b][:, 0:512],
                                       y[:, 512:1024], Alu.add), inc="sV")
                bank_free[2 * g] = ("sV", p.n("sV"))
                bank_free[2 * g + 1] = ("sV", p.n("sV"))
                p.wait("scalar", "sV")
                rows0 = tau * TT + g * 128
                p.dma("scalar", y_h[rows0:rows0 + 128, :], ybuf, "qR")
                store_done[idx] = qfence("scalar")
            for sslot in range(4):
                wg_slot_free.pop(sslot, None)
            p.wait("sync", "fR", p.n("fR"))

        p.wait("scalar", "qR", p.n("qR"))

        block = es.enter_context(nc.Block())
        p.emit(block)

    return nc


_NC_CACHE = None


def kernel(L0_attn_x, L0_attn_Wz, L0_attn_bz, L0_attn_Wh, L0_attn_bh,
           L0_attn_log_D, L0_ffn_w_gate, L0_ffn_w_up, L0_ffn_w_down):
    global _NC_CACHE
    x = np.ascontiguousarray(np.asarray(L0_attn_x, dtype=np.float32)
                             ).reshape(16384, 1024)
    wz = np.ascontiguousarray(np.asarray(L0_attn_Wz, dtype=np.float32))
    wh = np.ascontiguousarray(np.asarray(L0_attn_Wh, dtype=np.float32))
    bz = np.asarray(L0_attn_bz, dtype=np.float32)
    bh = np.asarray(L0_attn_bh, dtype=np.float32)
    ld = np.asarray(L0_attn_log_D, dtype=np.float32)
    wg = np.ascontiguousarray(np.asarray(L0_ffn_w_gate, dtype=np.float32))
    wu = np.ascontiguousarray(np.asarray(L0_ffn_w_up, dtype=np.float32))
    wd = np.ascontiguousarray(np.asarray(L0_ffn_w_down, dtype=np.float32))

    bzp = np.ascontiguousarray(bz.reshape(8, 128).T)
    bhp = np.ascontiguousarray(bh.reshape(8, 128).T)
    ldt = np.ascontiguousarray(np.tile(ld[None, :], (128, 1)))
    idf = np.eye(128, dtype=np.float32)
    idb = np.eye(128, dtype=ml_dtypes.bfloat16)

    units = [wg[i * 128:(i + 1) * 128, :] for i in range(22)] + \
            [wu[i * 128:(i + 1) * 128, :] for i in range(22)] + \
            [np.zeros((128, 1024), np.float32)] * 4

    if _NC_CACHE is None:
        _NC_CACHE = build_nc()
    nc = _NC_CACHE

    in_maps = []
    for c in range(NCORES):
        onehot = np.zeros(8, np.float32)
        if c > 0:
            onehot[c - 1] = 1.0
        sel = np.ascontiguousarray(np.tile(onehot[None, :], (128, 8)))
        wsh1 = np.ascontiguousarray(
            np.concatenate(units[6 * c:6 * c + 6], axis=0))
        if c < 6:
            wsl = wd[:, c * 384:(c + 1) * 384]
        else:
            wsl = np.concatenate(
                [wd[:, 2304 + (c - 6) * 256: 2304 + (c - 5) * 256],
                 np.zeros((1024, 128), np.float32)], axis=1)
        wsh2 = np.ascontiguousarray(wsl)
        in_maps.append({
            "x": np.ascontiguousarray(x[c * TC:(c + 1) * TC, :]),
            "wz": wz, "wh": wh, "bzp": bzp, "bhp": bhp, "ldt": ldt,
            "wsh1": wsh1, "wsh2": wsh2, "idf": idf, "idb": idb, "sel": sel,
        })

    res = run_bass_kernel_spmd(nc, in_maps, core_ids=list(range(NCORES)))
    out = np.concatenate([res.results[c]["y"] for c in range(NCORES)], axis=0)
    return out.reshape(1, 16384, 1024)
